# version 02 (final)
# baseline (speedup 1.0000x reference)
import contextlib
import numpy as np
import concourse.bass as bass
import concourse.mybir as mybir

F32 = mybir.dt.float32
BF16 = mybir.dt.bfloat16
FP16 = mybir.dt.float16
ALU = mybir.AluOpType
ACT = mybir.ActivationFunctionType
AX = mybir.AxisListType

D = 1024
DI = 2048
NST = 16
HEADS = 4
DK = 256
DV = 512
H4 = 4096
MAGIC = 12582912.0


class Cfg:
    def __init__(self, L=4096, Tc=512, blocks=(0, 1, 2, 3), debug_taps=False):
        self.L = L
        self.Tc = Tc
        self.nch = L // Tc
        self.blocks = blocks
        self.debug_taps = debug_taps


ENGS = ("sync", "tensor", "vector", "scalar", "gpsimd")


class Item:
    __slots__ = ("eng", "fn", "is_dma", "inc_sem", "inc_val", "waits", "resolved", "label", "_gid")
    _next_gid = [0]

    def __init__(self, eng, fn, is_dma, label=""):
        self._gid = Item._next_gid[0]
        Item._next_gid[0] += 1
        self.eng = eng
        self.fn = fn
        self.is_dma = is_dma
        self.inc_sem = None
        self.inc_val = 16 if is_dma else 1
        self.waits = []
        self.resolved = None
        self.label = label


class Prog:
    def __init__(self, nc, es):
        self.nc = nc
        self.items = {e: [] for e in ENGS}
        self.sems = {}
        self.es = es
        for e in ENGS:
            self.sems[e] = es.enter_context(nc.semaphore("c_" + e))
        for e in ("sync", "scalar", "gpsimd", "vector"):
            self.sems["d_" + e] = es.enter_context(nc.semaphore("d_" + e))
        for s in ("dq_r0", "dq_r1", "dq_r2", "dq_s0", "dq_s1",
                  "dq_c0", "dq_c1", "dq_c2", "dq_io", "dq_pro"):
            self.sems[s] = es.enter_context(nc.semaphore(s))

    def op(self, eng, fn, waits=(), label=""):
        it = Item(eng, fn, False, label)
        for w in waits:
            if w is not None:
                if isinstance(w, (list, tuple)):
                    it.waits.extend(x for x in w if x is not None)
                else:
                    it.waits.append(w)
        self.items[eng].append(it)
        return it

    def dma(self, eng, fn, waits=(), label="", sem=None):
        it = Item(eng, fn, True, label)
        if sem is not None:
            it.inc_sem = sem
        for w in waits:
            if w is not None:
                if isinstance(w, (list, tuple)):
                    it.waits.extend(x for x in w if x is not None)
                else:
                    it.waits.append(w)
        self.items[eng].append(it)
        return it

    def _mark(self, it):
        if it.inc_sem is None:
            if it.is_dma:
                it.inc_sem = "d_" + it.eng
            else:
                it.inc_sem = it.eng

    def need(self, it):
        self._mark(it)
        return it

    PARANOID = 0
    DMA_CHAIN = 0

    def finalize(self, block):
        if self.DMA_CHAIN:
            prev_dma = {}
            for e in ENGS:
                for it in self.items[e]:
                    if it.is_dma:
                        if e in prev_dma:
                            self._mark(prev_dma[e])
                            it.waits.append(prev_dma[e])
                        prev_dma[e] = it
        if self.PARANOID:
            seq = []
            for e in ENGS:
                for i, it in enumerate(self.items[e]):
                    seq.append(it)
            allitems = sorted(seq, key=lambda it: it._gid)
            prev = {e: None for e in ENGS}
            for it in allitems:
                for e2 in ENGS:
                    if prev[e2] is not None and prev[e2] is not it:
                        self._mark(prev[e2])
                        it.waits.append(prev[e2])
                prev[it.eng] = it
        counts = {k: 0 for k in self.sems}
        for e in ENGS:
            for it in self.items[e]:
                for w in it.waits:
                    assert w.inc_sem is not None, f"wait on unmarked item {w.label}"
        for e in ENGS:
            for it in self.items[e]:
                if it.inc_sem is not None:
                    counts[it.inc_sem] += it.inc_val
                    it.resolved = (it.inc_sem, counts[it.inc_sem])

        def emit(e_name, eng):
            waited = {}
            for it in self.items[e_name]:
                for w in it.waits:
                    sem_name, val = w.resolved
                    if sem_name == e_name and not w.is_dma:
                        continue
                    if waited.get(sem_name, -1) >= val:
                        continue
                    eng.wait_ge(self.sems[sem_name], val)
                    waited[sem_name] = val
                inst = it.fn(eng)
                if it.inc_sem is not None:
                    assert inst is not None, f"inc-marked item returned None: {it.label}"
                    inst.then_inc(self.sems[it.inc_sem], it.inc_val)

        @block.sync
        def _(e):
            emit("sync", e)

        @block.tensor
        def _(e):
            emit("tensor", e)

        @block.vector
        def _(e):
            emit("vector", e)

        @block.scalar
        def _(e):
            emit("scalar", e)

        @block.gpsimd
        def _(e):
            emit("gpsimd", e)


def ts_(i, n):
    return slice(i * n, (i + 1) * n)



def pack_inputs(ins, cfg, core):
    import ml_dtypes
    bf = ml_dtypes.bfloat16
    f32 = np.float32
    o = {}
    x = np.ascontiguousarray(ins["L0_attn_x"][core, :cfg.L, :].T).astype(f32, copy=False)
    o["x_fm"] = x
    o["w_inproj"] = np.ascontiguousarray(ins["L0_attn_in_proj_w"].T)
    cw = ins["L0_attn_conv_w"][:, 0, :]
    o["conv_w"] = np.ascontiguousarray(cw.reshape(16, 128, 4).transpose(1, 0, 2).reshape(128, 64))
    o["conv_b"] = np.ascontiguousarray(ins["L0_attn_conv_b"].reshape(16, 128).T)
    o["w_xproj"] = np.ascontiguousarray(ins["L0_attn_x_proj_w"].T)
    o["dt_w"] = np.ascontiguousarray(ins["L0_attn_dt_w"][:, 0].reshape(16, 128).T)
    o["dt_b"] = np.ascontiguousarray(ins["L0_attn_dt_b"].reshape(16, 128).T)
    o["A_log"] = np.ascontiguousarray(ins["L0_attn_A_log"].reshape(16, 128, 16).transpose(1, 0, 2).reshape(128, 256))
    o["D_par"] = np.ascontiguousarray(ins["L0_attn_D_param"].reshape(16, 128).T)
    o["w_outproj"] = np.ascontiguousarray(ins["L0_attn_out_proj_w"].T)
    o["w_W1"] = np.ascontiguousarray(ins["L0_ffn_W1"].T)
    o["b1"] = np.ascontiguousarray(ins["L0_ffn_b1"].reshape(32, 128).T)
    o["w_W2"] = np.ascontiguousarray(ins["L0_ffn_W2"].T)
    o["b2"] = np.ascontiguousarray(ins["L0_ffn_b2"].reshape(8, 128).T)
    o["scale_rep"] = np.full((128, 1), np.float32(ins["L0_ffn_scale"]), f32)
    o["w_Wq"] = np.ascontiguousarray(ins["L1_attn_Wq"])
    o["w_Wk"] = np.ascontiguousarray(ins["L1_attn_Wk"])
    o["w_Wv"] = np.ascontiguousarray(ins["L1_attn_Wv"])
    o["w_Wg"] = np.ascontiguousarray(ins["L1_attn_Wg"])
    o["wgk1"] = np.ascontiguousarray(ins["L1_attn_Wgk1"])
    o["wgk2"] = np.ascontiguousarray(ins["L1_attn_Wgk2"])
    o["bgk2"] = np.ascontiguousarray(ins["L1_attn_bgk2"].reshape(8, 128).T)
    o["w_Wo"] = np.ascontiguousarray(ins["L1_attn_Wo"])
    o["gnw_rep"] = np.broadcast_to(ins["L1_attn_g_norm_weight"][None, :], (128, 512)).astype(bf).copy()
    o["w_gate"] = np.ascontiguousarray(ins["L1_ffn_w_gate"].T)
    o["w_up"] = np.ascontiguousarray(ins["L1_ffn_w_up"].T)
    o["w_down"] = np.ascontiguousarray(ins["L1_ffn_w_down"].T)
    o["ones1"] = np.ones((1, 128), bf)
    o["ones1f"] = np.ones((1, 128), f32)
    ind16 = np.zeros((16, 2048), bf)
    for n in range(16):
        ind16[n, n * 128:(n + 1) * 128] = 1
    o["ind16"] = ind16
    ind4 = np.zeros((4, 512), bf)
    for n in range(4):
        ind4[n, n * 128:(n + 1) * 128] = 1
    o["ind4"] = ind4
    o["ident"] = np.eye(128, dtype=bf)
    o["ident_f"] = np.eye(128, dtype=f32)
    o["mask_ut"] = np.triu(np.ones((128, 128))).astype(bf)
    for k, v in o.items():
        o[k] = np.ascontiguousarray(v)
    return o


DT_MAP = None


def _np2dt(npdt):
    import ml_dtypes
    m = {np.dtype(np.float32): F32, np.dtype(np.float16): FP16,
         np.dtype(ml_dtypes.bfloat16): BF16, np.dtype(np.int32): mybir.dt.int32}
    return m[np.dtype(npdt)]



class WRing:
    SLOT_COLS = 2048
    NSLOT = 3
    NO_CACHE = 0

    def __init__(self, bld):
        self.b = bld
        nc = bld.nc
        es = bld.es
        self.slots = [es.enter_context(nc.sbuf_tensor(f"wr{i}", [128, self.SLOT_COLS], BF16))
                      for i in range(self.NSLOT)]
        self.stag = [es.enter_context(nc.sbuf_tensor(f"wstag{i}", [128, 512], F32))
                     for i in range(2)]
        self.idx = 0
        self.stag_idx = 0
        self.cache_done = {}
        self.slot_free = [None] * self.NSLOT
        self.stag_free = [None, None]
        self.cast_rr = 0

    def load(self, mat, k0, kk, m0, mt, first_chunk, quant=None, extra_waits=()):
        if self.NO_CACHE:
            first_chunk = True
        """Schedule load of granule: rows k0..k0+kk*128, cols m0*128..+mt*128.
        mat: name string. Returns (view_fn, ready_token). view(kki, mi) -> lhsT AP."""
        b = self.b
        p = b.prog
        i = self.idx
        self.idx += 1
        slot = self.slots[i % self.NSLOT]
        ncols = kk * mt * 128
        assert ncols <= self.SLOT_COLS
        free_tok = self.slot_free[i % self.NSLOT]
        src_f32 = b.dram[mat]
        cache = b.dram[mat + "_c"]
        Mtot = src_f32.shape[1]
        if first_chunk:
            casts = []
            qk = 512 // (mt * 128)
            nq = (kk + qk - 1) // qk
            for h in range(nq):
                kqa = h * qk
                kqb = min(kk, kqa + qk)
                ncq = (kqb - kqa) * mt * 128
                st = self.stag[self.stag_idx % 2]
                st_free = self.stag_free[self.stag_idx % 2]
                self.stag_idx += 1
                srcap = src_f32.rearrange("(kk p) M -> p kk M", p=128)
                d = p.dma("sync", (lambda e, st=st, srcap=srcap, kqa=kqa, kqb=kqb, m0=m0, mt=mt, k0=k0, ncq=ncq:
                           e.dma_start(out=st[:, :ncq].rearrange("p (kk m) -> p kk m", kk=kqb - kqa),
                                       in_=srcap[:, k0 // 128 + kqa:k0 // 128 + kqb, m0 * 128:(m0 + mt) * 128])),
                          waits=([st_free] + (free_tok or []) + list(extra_waits)), label=f"wload{mat}",
                          sem="dq_s%d" % ((self.stag_idx - 1) % 2))
                p.need(d)
                if quant is not None:
                    sw = self.b.sb["swcol"]
                    off = kqa * mt * 128
                    c1 = p.op("vector", (lambda e, st=st, n=ncq, sw=sw, quant=quant:
                               e.tensor_scalar(out=st[:, :n], in0=st[:, :n],
                                               scalar1=sw[:, quant:quant + 1], scalar2=None,
                                               op0=ALU.mult)),
                              waits=[d] + (free_tok or []) + list(extra_waits))
                    c2 = p.op("vector", (lambda e, st=st, n=ncq:
                               e.tensor_scalar(out=st[:, :n], in0=st[:, :n],
                                               scalar1=MAGIC, scalar2=None, op0=ALU.add)))
                    c3 = p.op("vector", (lambda e, st=st, n=ncq:
                               e.tensor_scalar(out=st[:, :n], in0=st[:, :n],
                                               scalar1=-MAGIC, scalar2=None, op0=ALU.add)))
                    c = p.op("vector", (lambda e, slot=slot, st=st, off=off, n=ncq:
                               e.tensor_scalar(out=slot[:, off:off + n], in0=st[:, :n],
                                               scalar1=1.0, scalar2=-1.0,
                                               op0=ALU.min, op1=ALU.max)))
                    p.need(c)
                else:
                    eng = ("vector", "scalar", "gpsimd")[self.cast_rr % 3]
                    self.cast_rr += 1
                    off = kqa * mt * 128
                    if eng == "scalar":
                        c = p.op(eng, (lambda e, slot=slot, st=st, off=off, n=ncq:
                                       e.copy(slot[:, off:off + n], st[:, :n])),
                                 waits=[d] + (free_tok or []))
                    else:
                        c = p.op(eng, (lambda e, slot=slot, st=st, off=off, n=ncq:
                                       e.tensor_copy(slot[:, off:off + n], st[:, :n])),
                                 waits=[d] + (free_tok or []))
                    p.need(c)
                self.stag_free[(self.stag_idx - 1) % 2] = c
                casts.append(c)
            if self.NO_CACHE:
                ready = casts
                cache_tok = None
            else:
                cap = cache.rearrange("(kk p) M -> p kk M", p=128)
                wtok = p.dma("sync", (lambda e, slot=slot, cap=cap, k0=k0, kk=kk, m0=m0, mt=mt:
                              e.dma_start(out=cap[:, k0 // 128:k0 // 128 + kk, m0 * 128:(m0 + mt) * 128],
                                          in_=slot[:, :kk * mt * 128].rearrange("p (kk m) -> p kk m", kk=kk))),
                             waits=casts, label=f"wcache{mat}", sem="dq_c%d" % (i % self.NSLOT))
                p.need(wtok)
                ready = casts
                cache_tok = wtok
                self.cache_done.setdefault(mat, {})[i % self.NSLOT] = wtok
        else:
            cap = cache.rearrange("(kk p) M -> p kk M", p=128)
            d = p.dma("sync", (lambda e, slot=slot, cap=cap, k0=k0, kk=kk, m0=m0, mt=mt:
                       e.dma_start(out=slot[:, :kk * mt * 128].rearrange("p (kk m) -> p kk m", kk=kk),
                                   in_=cap[:, k0 // 128:k0 // 128 + kk, m0 * 128:(m0 + mt) * 128])),
                      waits=(free_tok or []) + list(self.cache_done.get(mat, {}).values()),
                      label=f"wload{mat}", sem="dq_r%d" % (i % self.NSLOT))
            p.need(d)
            ready = [d]
            cache_tok = None

        myslot = i % self.NSLOT

        def view(kki, mi):
            off = (kki * mt + mi) * 128
            return slot[:, off:off + 128]

        def release(tok):
            self.slot_free[myslot] = [tok] + ([cache_tok] if cache_tok is not None else [])
        return view, ready, release



class Builder:
    def __init__(self, cfg):
        self.cfg = cfg
        self.nc = bass.Bass("TRN2", target_bir_lowering=False, debug=False)
        self.es = contextlib.ExitStack()
        self.dram = {}
        self.sb = {}
        self.ps = {}

    def dram_in(self, name, shape, dt=F32):
        self.dram[name] = self.nc.dram_tensor(name, list(shape), dt, kind="ExternalInput").ap()

    def dram_out(self, name, shape, dt=F32):
        self.dram[name] = self.nc.dram_tensor(name, list(shape), dt, kind="ExternalOutput").ap()

    def dram_int(self, name, shape, dt=BF16):
        self.dram[name] = self.nc.dram_tensor(name, list(shape), dt, kind="Internal").ap()

    def sbuf(self, name, shape, dt):
        self.sb[name] = self.es.enter_context(self.nc.sbuf_tensor(name, list(shape), dt))
        return self.sb[name]

    def psum(self, name, shape, dt=F32):
        self.ps[name] = self.es.enter_context(self.nc.psum_tensor(name, list(shape), dt))
        return self.ps[name]

    def declare(self):
        cfg = self.cfg
        L = cfg.L
        self.dram_in("x_fm", [D, L])
        self.dram_out("out_fm", [D, L])
        for nm, shape in [("w_inproj", [D, 2 * DI]), ("w_outproj", [DI, D]),
                          ("w_W1", [D, H4]), ("w_W2", [H4, D]),
                          ("w_Wq", [D, D]), ("w_Wk", [D, D]), ("w_Wv", [D, 2048]), ("w_Wg", [D, 2048]),
                          ("w_Wo", [2048, D]),
                          ("w_gate", [D, H4]), ("w_up", [D, H4]), ("w_down", [H4, D])]:
            self.dram_in(nm, shape)
            self.dram_int(nm + "_c", shape, BF16)
        for nm, shape, dt in [("conv_w", (128, 64), F32), ("conv_b", (128, 16), F32),
                              ("w_xproj", (DI, 33), F32), ("dt_w", (128, 16), F32),
                              ("dt_b", (128, 16), F32), ("A_log", (128, 256), F32),
                              ("D_par", (128, 16), F32),
                              ("b1", (128, 32), F32), ("b2", (128, 8), F32), ("scale_rep", (128, 1), F32),
                              ("wgk1", (D, 16), F32), ("wgk2", (16, D), F32), ("bgk2", (128, 8), F32),
                              ("gnw_rep", (128, 512), BF16),
                              ("ones1", (1, 128), BF16), ("ones1f", (1, 128), F32), ("ind16", (16, 2048), BF16),
                              ("ind4", (4, 512), BF16), ("ident", (128, 128), BF16),
                              ("ident_f", (128, 128), F32), ("mask_ut", (128, 128), BF16)]:
            self.dram_in(nm, shape, dt)
        if cfg.debug_taps:
            self.dram_out("tap_h1", [128, 32 * 512], BF16)
            self.dram_out("tap_prep", [128, 4 * 512], BF16)
            self.dram_out("tap_grep", [128, 16 * 512], BF16)
            self.dram_out("tap_lora", [128, 128])
            self.dram_out("tap_prow", [16, 512], BF16)
            self.dram_out("tap_xi", [128, 8 * 512], BF16)
            self.dram_out("tap_gu", [128, 8192], BF16)
            self.dram_out("tap_sw", [128, 12])
            self.dram_out("tap_xi2", [128, 8192], BF16)
            self.dram_out("tap_ofm", [128, 8192], BF16)
            self.dram_out("tap_qk", [128, 8192], BF16)
            self.dram_out("tap_gk", [128, 4096], FP16)

        self.sbuf("xres", (128, 8 * 512), F32)
        self.sbuf("xn", (128, 8 * 512), BF16)
        self.sbuf("A1", (128, 8192), BF16)
        self.sbuf("A2", (128, 8192), BF16)
        self.sbuf("A3", (128, 8192), BF16)
        self.sbuf("A4", (128, 8192), BF16)
        self.sbuf("A5", (128, 4096), FP16)
        self.sbuf("yg", (128, 8192), BF16)
        self.sbuf("SCRF", (128, 8 * 512), F32)
        self.sbuf("SCRB", (128, 10 * 512), BF16)
        self.sbuf("xin", (128, 515), BF16)
        self.sbuf("tails", (128, 16 * 4), BF16)
        self.sbuf("mstate", (128, 256), BF16)
        self.sbuf("Sstate", (128, 8 * 512), BF16)
        self.sbuf("Sst32", (128, 8 * 512), F32)
        self.sbuf("lora", (128, 8 * 16), F32)
        self.sbuf("lora_b", (128, 64), BF16)
        self.sbuf("e2r", (128, 32), F32)
        self.sbuf("strow", (1, 1024), F32)
        self.sbuf("strow_b", (1, 1024), BF16)
        self.sbuf("colsc", (128, 16), F32)
        self.sbuf("cpool", (128, 16), F32)
        self.sbuf("wsum", (128, 4), F32)
        self.sbuf("swcol", (128, 12), F32)
        self.sbuf("wstat", (128, 64), F32)
        self.sbuf("c_conv_w", (128, 64), F32)
        self.sbuf("c_conv_b", (128, 16), F32)
        self.sbuf("c_xproj", (128, 16 * 33), BF16)
        self.sbuf("c_dt_w", (128, 16), F32)
        self.sbuf("c_dt_b", (128, 16), F32)
        self.sbuf("c_A", (128, 256), F32)
        self.sbuf("c_D", (128, 16), F32)
        self.sbuf("c_b1", (128, 32), F32)
        self.sbuf("c_b2", (128, 8), F32)
        self.sbuf("c_scale", (128, 1), F32)
        self.sbuf("c_wgk1", (128, 8 * 16), BF16)
        self.sbuf("c_wgk1_f", (128, 8 * 16), F32)
        self.sbuf("c_wgk2", (16, 1024), BF16)
        self.sbuf("c_bgk2", (128, 8), F32)
        self.sbuf("c_gnw", (128, 512), BF16)
        self.sbuf("c_ones1", (1, 128), BF16)
        self.sbuf("c_onescol_b", (128, 1), BF16)
        self.sbuf("c_ones1f", (1, 128), F32)
        self.sbuf("c_ind16", (16, 2048), BF16)
        self.sbuf("c_ind4", (4, 512), BF16)
        self.sbuf("c_ident", (128, 128), BF16)
        self.sbuf("c_ident_f", (128, 128), F32)
        self.sbuf("c_mask", (128, 128), BF16)

        for i in range(7):
            self.psum(f"p{i}", (128, 512), F32)
        self.psum("p7", (128, 1024), BF16)

    def ccol(self, value):
        if value not in self._ccol_map:
            idx = len(self._ccol_map)
            assert idx < 16
            self._ccol_map[value] = idx
            t = self.need(self.V((lambda e, idx=idx, value=value:
                                  e.memset(self.sb["cpool"][:, idx:idx + 1], float(value)))))
            self._ccol_tok[value] = t
        idx = self._ccol_map[value]
        return self.sb["cpool"][:, idx:idx + 1], self._ccol_tok[value]

    def V(self, fn, waits=(), label=""):
        return self.prog.op("vector", fn, waits, label)

    def S(self, fn, waits=(), label=""):
        return self.prog.op("scalar", fn, waits, label)

    def G(self, fn, waits=(), label=""):
        return self.prog.op("gpsimd", fn, waits, label)

    def T(self, fn, waits=(), label=""):
        return self.prog.op("tensor", fn, waits, label)

    def need(self, it):
        return self.prog.need(it)

    def mm_group(self, psum_ap, pairs, waits=(), label=""):
        n = len(pairs)
        last = None
        for i, pr in enumerate(pairs):
            lf, rf = pr[0], pr[1]
            w = list(waits) if i == 0 else []
            if len(pr) > 2 and pr[2]:
                w = w + list(pr[2])
            last = self.T((lambda e, lf=lf, rf=rf, i=i, n=n:
                           e.matmul(psum_ap, lf(), rf(), start=(i == 0), stop=(i == n - 1))),
                          waits=w, label=label)
        return last

    def prologue(self):
        p = self.prog
        sb = self.sb
        dr = self.dram
        self.ld = {}
        self.scrub_tok = None
        if getattr(self.cfg, "scrub", False):
            last = None
            only = getattr(self.cfg, "scrub_only", None)
            items = list(self.sb.items()) + list(self.ps.items())
            rng = getattr(self.cfg, "scrub_range", None)
            for nm, t in items:
                v = float(getattr(self.cfg, "scrub_val", 0.7))
                if only is not None and nm not in only:
                    v = 0.0
                if nm == "SCRB" and rng is not None and v != 0.0:
                    last = self.need(self.V((lambda e, t=t: e.memset(t[:, :], 0.0))))
                    if rng[0] < 0:
                        last = self.need(self.V((lambda e, t=t, v=v: e.memset(t[:, 6 * 512 + 511:6 * 512 + 512], v))))
                        last = self.need(self.V((lambda e, t=t, v=v: e.memset(t[:, 7 * 512 + 511:7 * 512 + 512], v))))
                    else:
                        last = self.need(self.V((lambda e, t=t, v=v, rng=rng: e.memset(t[:, rng[0]:rng[1]], v))))
                else:
                    last = self.need(self.V((lambda e, t=t, v=v: e.memset(t[:, :], v))))
            self.scrub_tok = last
            self.ring.slot_free = [[last]] * self.ring.NSLOT
            self.ring.stag_free = [last, last]
        loads = [("c_conv_w", "conv_w"), ("c_conv_b", "conv_b"),
                 ("c_dt_w", "dt_w"), ("c_dt_b", "dt_b"), ("c_A", "A_log"), ("c_D", "D_par"),
                 ("c_b1", "b1"), ("c_b2", "b2"), ("c_scale", "scale_rep"),
                 ("c_wgk1_f", "wgk1"), ("c_bgk2", "bgk2"),
                 ("c_gnw", "gnw_rep"), ("c_ones1", "ones1"), ("c_ones1f", "ones1f"), ("c_ind16", "ind16"),
                 ("c_ind4", "ind4"), ("c_ident", "ident"), ("c_ident_f", "ident_f"),
                 ("c_mask", "mask_ut")]
        prev = self.scrub_tok
        for dst, src in loads:
            ap = dr[src]
            if dst == "c_wgk1_f":
                ap = ap.rearrange("(kk p) M -> p kk M", p=128)
                tok = p.dma("sync", (lambda e, d=sb[dst], ap=ap: e.dma_start(
                    out=d[:, :].rearrange("p (kk M) -> p kk M", kk=8), in_=ap)),
                    waits=[prev], label="ld_" + dst, sem="dq_pro")
            else:
                tok = p.dma("sync", (lambda e, d=sb[dst], ap=ap: e.dma_start(out=d[:, :], in_=ap)),
                            waits=[prev], label="ld_" + dst, sem="dq_pro")
            self.ld[dst] = p.need(tok)
            prev = self.ld[dst]
        xpap = self.dram["w_xproj"].rearrange("(kk p) M -> p kk M", p=128)
        xd0 = self.prog.dma("sync", lambda e: e.dma_start(
            out=self.ring.stag[0][:, 0:264].rearrange("p (kk M) -> p kk M", kk=8),
            in_=xpap[:, 0:8, :]), waits=[prev], label="ld_xp0", sem="dq_s0")
        xd1 = self.prog.dma("sync", lambda e: e.dma_start(
            out=self.ring.stag[1][:, 0:264].rearrange("p (kk M) -> p kk M", kk=8),
            in_=xpap[:, 8:16, :]), waits=[prev], label="ld_xp1", sem="dq_s1")
        xc0 = self.V(lambda e: e.tensor_copy(sb["c_xproj"][:, 0:264], self.ring.stag[0][:, 0:264]),
                     waits=[self.prog.need(xd0)])
        xc1 = self.V(lambda e: e.tensor_copy(sb["c_xproj"][:, 264:528], self.ring.stag[1][:, 0:264]),
                     waits=[self.prog.need(xd1)])
        self.ring.stag_free = [self.need(xc0), self.need(xc1)]
        self.ld["c_xproj"] = self.ring.stag_free[1]
        self.ld["c_wgk1"] = self.need(self.V(lambda e: e.tensor_copy(sb["c_wgk1"][:, :], sb["c_wgk1_f"][:, :]),
                                             waits=[self.ld["c_wgk1_f"]]))
        wg2a = self.prog.dma("sync", lambda e: e.dma_start(out=self.ring.stag[0][0:16, :],
                                                           in_=self.dram["wgk2"][:, 0:512]),
                             waits=[self.ring.stag_free[0]], label="ld_wgk2a", sem="dq_s0")
        wg2b = self.prog.dma("sync", lambda e: e.dma_start(out=self.ring.stag[1][0:16, :],
                                                           in_=self.dram["wgk2"][:, 512:1024]),
                             waits=[self.ring.stag_free[1]], label="ld_wgk2b", sem="dq_s1")
        wc0 = self.V(lambda e: e.tensor_copy(sb["c_wgk2"][:, 0:512], self.ring.stag[0][0:16, :]),
                     waits=[self.prog.need(wg2a)])
        wc1 = self.V(lambda e: e.tensor_copy(sb["c_wgk2"][:, 512:1024], self.ring.stag[1][0:16, :]),
                     waits=[self.prog.need(wg2b)])
        self.ring.stag_free = [self.need(wc0), self.need(wc1)]
        self.ld["c_wgk2"] = self.ring.stag_free[1]
        a1 = self.S(lambda e: e.activation(sb["c_A"][:, :], sb["c_A"][:, :], ACT.Exp), waits=[self.ld["c_A"]])
        self.ld["c_A"] = self.need(self.V(lambda e: e.tensor_scalar_mul(sb["c_A"][:, :], sb["c_A"][:, :], -1.0),
                                          waits=[self.need(a1)]))
        self.tok_state0 = self.need(self.V(lambda e: e.memset(sb["mstate"][:, :], 0.0)))
        self.tok_tails0 = self.need(self.V(lambda e: e.memset(sb["tails"][:, :], 0.0)))
        self.tok_S0 = self.need(self.G(lambda e: e.memset(sb["Sstate"][:, :], 0.0)))
        tokS32 = self.G(lambda e: e.memset(sb["Sst32"][:, :], 0.0))
        self.tok_ones1f = self.need(self.V(lambda e: e.memset(sb["colsc"][:, 15:16], 1.0)))
        self.tok_onescol = self.need(self.V(lambda e: e.memset(sb["c_onescol_b"][:, :], 1.0)))
        if 3 in self.cfg.blocks:
            self._bitnet_scale_pass()

    def _bitnet_scale_pass(self):
        p = self.prog
        sb = self.sb
        mats = ["w_gate", "w_up", "w_down"]
        zero = self.need(self.V(lambda e: e.memset(sb["wsum"][:, :], 0.0)))
        acc_tok = {m: zero for m in mats}
        stag = self.ring.stag
        stag_free = list(self.ring.stag_free)
        idx = 0
        tile_toks = {m: [] for m in mats}
        ntiles = {}
        for j, m in enumerate(mats):
            ap = self.dram[m]
            K, M = ap.shape
            ti = 0
            for kk in range(K // 128):
                for mm in range(M // 512):
                    s = idx % 2
                    d = p.dma("sync", (lambda e, st=stag[s], ap=ap, kk=kk, mm=mm:
                               e.dma_start(out=st[:, :],
                                           in_=ap[kk * 128:(kk + 1) * 128, mm * 512:(mm + 1) * 512])),
                              waits=[stag_free[s]], label="bqld", sem="dq_s%d" % s)
                    p.need(d)
                    r = self.V((lambda e, st=stag[s], ti=ti: e.tensor_reduce(
                        sb["wstat"][:, ti:ti + 1], st[:, :], AX.X, ALU.add, apply_absolute_value=True)),
                        waits=[d], label="bqred")
                    stag_free[s] = self.need(r)
                    tile_toks[m].append(stag_free[s])
                    idx += 1
                    ti += 1
            ntiles[m] = ti
            spc = self.V(lambda e: e.memset(sb["strow_b"][0:1, :], 0.0), label="bqspacer")
            a = self.V((lambda e, j=j, ti=ti: e.tensor_reduce(
                sb["wsum"][:, j:j + 1], sb["wstat"][:, 0:ti], AX.X, ALU.add)),
                waits=tile_toks[m], label="bqsum1")
            acc_tok[m] = self.need(a)
        for j, m in enumerate(mats):
            K, M = self.dram[m].shape
            nel = float(K * M)
            mm1 = self.T((lambda e, j=j: e.matmul(self.ps["p3"][0:1, 0:1], sb["wsum"][:, j:j + 1],
                                                  sb["colsc"][:, 15:16], start=True, stop=True)),
                         waits=[acc_tok[m], self.tok_ones1f], label="bqsum")
            t1 = self.V((lambda e, j=j, nel=nel: e.tensor_scalar(
                out=sb["strow"][0:1, 512 + j:513 + j], in0=self.ps["p3"][0:1, 0:1],
                scalar1=1.0 / nel, scalar2=1e-5, op0=ALU.mult, op1=ALU.max)), waits=[self.need(mm1)])
            spc = self.V(lambda e: e.memset(sb["strow_b"][0:1, :], 0.0), label="bqspacer2")
            mm2 = self.T((lambda e, j=j: e.matmul(self.ps["p4"][:, 0:1], sb["c_ones1f"][:, :],
                                                  sb["strow"][0:1, 512 + j:513 + j], start=True, stop=True)),
                         waits=[self.need(t1), self.ld["c_ones1f"]], label="bqbcast")
            mm2 = self.need(mm2)
            e1 = self.V((lambda e, j=j: e.tensor_copy(sb["swcol"][:, 3 * j:3 * j + 1],
                                                      self.ps["p4"][:, 0:1])), waits=[mm2])
            e2 = self.V((lambda e, j=j: e.reciprocal(sb["swcol"][:, 3 * j + 1:3 * j + 2],
                                                     self.ps["p4"][:, 0:1])), waits=[mm2])
            e3 = self.V((lambda e, j=j: e.tensor_scalar_mul(sb["swcol"][:, 3 * j + 2:3 * j + 3],
                                                            self.ps["p4"][:, 0:1], 1.0 / 127.0)),
                        waits=[mm2])
            acc_tok[m] = [self.need(e1), self.need(e2), self.need(e3)]
        self.tok_bq = {m: acc_tok[m] for m in mats}
        self.ring.stag_free = stag_free

    def ln(self, xn_tok_name="", lora_taps=False, waits=()):
        sb = self.sb
        ps = self.ps
        mm_mean = None
        mm_sq = None
        casts = []
        for r in range(8):
            sl = r % 2
            cb = self.V((lambda e, r=r, sl=sl: e.tensor_copy(
                sb["SCRB"][:, ts_(sl, 512)], sb["xres"][:, ts_(r, 512)])),
                waits=list(waits) + ([self.bank_free.get("p3")] if r == 0 else [self.casts2_free[sl]]),
                label="ln_cast")
            sq = self.S((lambda e, r=r, sl=sl: e.activation(
                sb["SCRB"][:, ts_(2 + sl, 512)], sb["SCRB"][:, ts_(sl, 512)], ACT.Square)),
                waits=[self.need(cb)])
            mm_mean = self.T((lambda e, r=r, sl=sl: e.matmul(
                ps["p3"][0:1, :], sb["c_onescol_b"][:, :], sb["SCRB"][:, ts_(sl, 512)],
                start=(r == 0), stop=(r == 7))), waits=[self.need(cb), self.tok_onescol], label="ln_mm_mean")
            mm_sq = self.T((lambda e, r=r, sl=sl: e.matmul(
                ps["p6"][0:1, :], sb["c_onescol_b"][:, :], sb["SCRB"][:, ts_(2 + sl, 512)],
                start=(r == 0), stop=(r == 7))),
                waits=[self.need(sq)] + ([self.bank_free.get("p6")] if r == 0 else []), label="ln_mm_sq")
            self.casts2_free[sl] = self.need(mm_sq)
        mm_mean = self.need(mm_mean)
        mm_sq = self.need(mm_sq)
        t_m = self.V(lambda e: e.tensor_scalar_mul(sb["strow_b"][0:1, 0:512], ps["p3"][0:1, :], 1.0 / 1024.0),
                     waits=[mm_mean, mm_sq])
        t_m = self.need(t_m)
        t_msq = self.V(lambda e: e.tensor_scalar_mul(sb["strow"][0:1, 0:512], ps["p6"][0:1, :], 1.0 / 1024.0),
                       waits=[mm_mean, mm_sq])
        t_m2 = self.S(lambda e: e.activation(sb["strow"][0:1, 512:1024], sb["strow_b"][0:1, 0:512], ACT.Square),
                      waits=[t_m])
        t_var = self.V(lambda e: e.tensor_sub(sb["strow"][0:1, 0:512], sb["strow"][0:1, 0:512],
                                              sb["strow"][0:1, 512:1024]),
                       waits=[self.need(t_m2), self.need(t_msq)])
        epsc, epst = self.ccol(1e-5)
        nhc, nht = self.ccol(-0.5)
        t_ln = self.S(lambda e: e.activation(sb["strow"][0:1, 512:1024], sb["strow"][0:1, 0:512],
                                             ACT.Ln, bias=epsc[0:1, :]), waits=[self.need(t_var), epst])
        t_rstd = self.S(lambda e: e.activation(sb["strow_b"][0:1, 512:1024], sb["strow"][0:1, 512:1024],
                                               ACT.Exp, scale=nhc[0:1, :]), waits=[self.need(t_ln), nht])
        mmb = self.T(lambda e: e.matmul(ps["p4"][:, :], sb["c_ones1"][:, :], sb["strow_b"][0:1, 0:512],
                                        start=True, stop=True),
                     waits=[t_m, self.bank_free.get("p4")], label="ln_bcast_m")
        mmr = self.T(lambda e: e.matmul(ps["p5"][:, :], sb["c_ones1"][:, :], sb["strow_b"][0:1, 512:1024],
                                        start=True, stop=True),
                     waits=[self.need(t_rstd), self.bank_free.get("p5")], label="ln_bcast_r")
        mmb = self.need(mmb)
        mmr = self.need(mmr)
        self.bank_free["p3"] = t_m
        self.bank_free["p6"] = t_msq
        toks = []
        last = None
        dst16 = getattr(self, "_ln_dst16", False)
        for r in range(8):
            sl = 4 + r % 2
            t1 = self.V((lambda e, r=r, sl=sl: e.tensor_sub(
                sb["SCRF"][:, ts_(sl, 512)], sb["xres"][:, ts_(r, 512)], ps["p4"][:, :])),
                waits=[mmb] + ([self.a5_free[r]] if dst16 else []), label="ln_sub")
            dstt = "A5" if dst16 else "xn"
            last = self.V((lambda e, r=r, sl=sl, dstt=dstt: e.tensor_mul(
                sb[dstt][:, ts_(r, 512)], sb["SCRF"][:, ts_(sl, 512)], ps["p5"][:, :])),
                waits=[mmr], label="ln_mul")
            toks.append(self.need(last))
            if lora_taps and r == 0:
                pass
        if lora_taps:
            self.lora_toks = []
            for r in range(8):
                t2 = self.V((lambda e, r=r: e.tensor_copy(
                    sb["lora"][:, r * 16:(r + 1) * 16],
                    sb["xn"][:, r * 512 + 1:r * 512 + 17])), waits=[toks[r]])
                toks.append(self.need(t2))
                self.lora_toks.append(toks[-1])
        self.bank_free["p4"] = toks[-1]
        self.bank_free["p5"] = toks[-1]
        return toks

    def l0_attn(self, c, xtok, ln_toks):
        cfg = self.cfg
        sb = self.sb
        ps = self.ps
        first = (c == 0)
        uready = [None] * 16
        zready = [None] * 16
        xin_free = [None, None]
        acc_free = [None, None]
        view = None
        gran_rel = None
        for m in range(32):
            if m % 2 == 0:
                if gran_rel is not None:
                    pass
                view, ready, release = self.ring.load("w_inproj", 0, 8, m, 2, first)
                gran_rel = release
            bank = "p%d" % (m % 2)
            pairs = []
            for kk in range(8):
                pairs.append(((lambda v=view, kk=kk, mi=m % 2: v(kk, mi)),
                              (lambda kk=kk: sb["xn"][:, ts_(kk, 512)]),
                              [ln_toks[kk]]))
            mm = self.mm_group(ps[bank][:, :], pairs,
                               waits=ready + [self.bank_free.get(bank)], label="inproj")
            mm = self.need(mm)
            if m % 2 == 1:
                gran_rel(mm)
            if m < 16:
                q = m
                sl = 0
                tc = self.V((lambda e, q=q, sl=sl: e.tensor_copy(
                    sb["xin"][:, sl * 515:sl * 515 + 3], sb["tails"][:, q * 4:q * 4 + 3])),
                    waits=[xin_free[sl], self.tail_tok[q]], label="tailcpy")
                ev = self.S((lambda e, q=q, sl=sl, bank=bank: e.copy(
                    sb["xin"][:, sl * 515 + 3:sl * 515 + 515], ps[bank][:, :])),
                    waits=[mm, xin_free[sl]], label="xin_evac")
                ev = self.need(ev)
                self.bank_free[bank] = ev
                tsv = self.V((lambda e, q=q, sl=sl: e.tensor_copy(
                    sb["tails"][:, q * 4:q * 4 + 3], sb["xin"][:, sl * 515 + 512:sl * 515 + 515])),
                    waits=[ev], label="tailsave")
                self.tail_tok[q] = self.need(tsv)
                asl = m % 2
                a0 = self.V((lambda e, q=q, sl=sl, asl=asl: e.tensor_scalar(
                    out=sb["SCRF"][:, ts_(asl, 512)], in0=sb["xin"][:, sl * 515:sl * 515 + 512],
                    scalar1=sb["c_conv_w"][:, q * 4 + 0:q * 4 + 1], scalar2=None, op0=ALU.mult)),
                    waits=[self.need(tc), ev, acc_free[asl], self.ld["c_conv_w"]], label="conv0")
                last = a0
                for j in range(1, 4):
                    last = self.V((lambda e, q=q, sl=sl, asl=asl, j=j: e.scalar_tensor_tensor(
                        out=sb["SCRF"][:, ts_(asl, 512)], in0=sb["xin"][:, sl * 515 + j:sl * 515 + 512 + j],
                        scalar=sb["c_conv_w"][:, q * 4 + j:q * 4 + j + 1], in1=sb["SCRF"][:, ts_(asl, 512)],
                        op0=ALU.mult, op1=ALU.add)), label=f"conv{j}")
                xin_free[sl] = self.need(last)
                usl = self.S((lambda e, q=q, asl=asl: e.activation(
                    sb["A1"][:, ts_(q, 512)], sb["SCRF"][:, ts_(asl, 512)], ACT.Silu,
                    bias=sb["c_conv_b"][:, q:q + 1])),
                    waits=[xin_free[sl], self.ld["c_conv_b"], self.u_free[q]], label="silu_u")
                uready[q] = self.need(usl)
                acc_free[asl] = uready[q]
            else:
                q = m - 16
                zv = self.S((lambda e, q=q, bank=bank: e.activation(
                    sb["A2"][:, ts_(q, 512)], ps[bank][:, :], ACT.Silu)),
                    waits=[mm, self.z_free[q]], label="silu_z")
                zready[q] = self.need(zv)
                self.bank_free[bank] = zready[q]
        return uready, zready

    def l0_attn_rest(self, c, uready, zready):
        sb = self.sb
        ps = self.ps
        def xp_group(psum_ap, c0, c1, bank_name):
            pairs = []
            for kk in range(16):
                pairs.append(((lambda kk=kk, c0=c0, c1=c1: sb["c_xproj"][:, kk * 33 + c0:kk * 33 + c1]),
                              (lambda kk=kk: sb["A1"][:, ts_(kk, 512)]),
                              [uready[kk]]))
            return self.need(self.mm_group(psum_ap, pairs,
                             waits=[self.ld["c_xproj"], self.bank_free.get(bank_name)], label="xproj"))

        mm_dt = xp_group(ps["p3"][0:1, :], 0, 1, "p3")
        mm_B = xp_group(ps["p6"][0:16, :], 1, 17, "p6")
        mm_C = xp_group(ps["p2"][0:16, :], 17, 33, "p2")
        ev_dt = self.need(self.S(lambda e: e.copy(sb["dtrow"][0:1, :], ps["p3"][0:1, :]),
                                 waits=[mm_dt, self.bsb_free]))
        ev_B = self.need(self.S(lambda e: e.copy(sb["Bsb"][:, :], ps["p6"][0:16, :]),
                                waits=[mm_B, self.bsb_free]))
        ev_C = self.need(self.S(lambda e: e.copy(sb["Csb"][:, :], ps["p2"][0:16, :]),
                                waits=[mm_C, self.bsb_free]))
        self.bank_free["p3"] = ev_dt
        self.bank_free["p6"] = ev_B
        self.bank_free["p2"] = ev_C
        mmd = self.need(self.T(lambda e: e.matmul(ps["p4"][:, :], sb["c_ones1"][:, :], sb["dtrow"][0:1, :],
                                                  start=True, stop=True),
                               waits=[ev_dt, self.bank_free.get("p4"), self.ld["c_ones1"]], label="bc_dt"))
        dtrep_tok = self.need(self.V(lambda e: e.tensor_copy(sb["dtrep"][:, :], ps["p4"][:, :]), waits=[mmd]))
        self.bank_free["p4"] = dtrep_tok
        brep_tok = [None] * 16
        crep_tok = [None] * 16
        for n in range(16):
            bank = "p%d" % (4 + n % 2)
            mmb = self.need(self.T((lambda e, n=n: e.matmul(
                ps["p%d" % (4 + n % 2)][:, :], sb["c_ind16"][:, n * 128:(n + 1) * 128], sb["Bsb"][:, :],
                start=True, stop=True)), waits=[ev_B, self.bank_free.get(bank), self.ld["c_ind16"]],
                label="bc_B"))
            evb = self.need(self.S((lambda e, n=n, b=bank: e.copy(sb["A3"][:, ts_(n, 512)], ps[b][:, :])),
                                   waits=[mmb, self.brep_free[n]]))
            brep_tok[n] = evb
            self.bank_free[bank] = evb
            mmc = self.need(self.T((lambda e, n=n: e.matmul(
                ps["p%d" % (4 + n % 2)][:, :], sb["c_ind16"][:, n * 128:(n + 1) * 128], sb["Csb"][:, :],
                start=True, stop=True)), waits=[ev_C, self.bank_free.get(bank)], label="bc_C"))
            evc = self.need(self.S((lambda e, n=n, b=bank: e.copy(sb["A4"][:, ts_(n, 512)], ps[b][:, :])),
                                   waits=[mmc, self.crep_free[n]]))
            crep_tok[n] = evc
            self.bank_free[bank] = evc
            if n == 15:
                self.bsb_free = mmc
        a_free = [None, None]
        b_free = [None, None]
        h_free = [None, None]
        tmp_free = [None, None]
        ygtok = [None] * 16
        for q in range(16):
            sd = 2 + (q & 1)
            stmp = 0 + (q & 1)
            d1 = self.V((lambda e, q=q, sd=sd: e.tensor_scalar(
                out=sb["SCRF"][:, ts_(sd, 512)], in0=sb["dtrep"][:, :],
                scalar1=sb["c_dt_w"][:, q:q + 1], scalar2=sb["c_dt_b"][:, q:q + 1],
                op0=ALU.mult, op1=ALU.add)),
                waits=[dtrep_tok, self.delta_free[q & 1], self.ld["c_dt_w"], self.ld["c_dt_b"]], label="dt")
            s1 = self.S((lambda e, sd=sd, stmp=stmp: e.activation(
                sb["SCRF"][:, ts_(stmp, 512)], sb["SCRF"][:, ts_(sd, 512)], ACT.Abs)),
                waits=[self.need(d1), self.scrt_free[q & 1]])
            s2 = self.S((lambda e, stmp=stmp: e.activation(
                sb["SCRF"][:, ts_(stmp, 512)], sb["SCRF"][:, ts_(stmp, 512)], ACT.Exp, scale=-1.0)))
            s3 = self.S((lambda e, stmp=stmp: e.activation(
                sb["SCRF"][:, ts_(stmp, 512)], sb["SCRF"][:, ts_(stmp, 512)], ACT.Ln, bias=1.0)))
            d2 = self.V((lambda e, sd=sd, stmp=stmp: e.scalar_tensor_tensor(
                out=sb["SCRF"][:, ts_(sd, 512)], in0=sb["SCRF"][:, ts_(sd, 512)], scalar=0.0,
                in1=sb["SCRF"][:, ts_(stmp, 512)], op0=ALU.max, op1=ALU.add)),
                waits=[self.need(s3)], label="softplus")
            delta_tok = self.need(d2)
            wv = self.V((lambda e, q=q, sd=sd: e.tensor_tensor(
                sb["SCRB"][:, ts_(4 + (q & 1), 512)], sb["SCRF"][:, ts_(sd, 512)],
                sb["A1"][:, ts_(q, 512)], ALU.mult)),
                waits=[delta_tok, uready[q], self.w_free[q & 1]], label="wbuild")
            w_tok = self.need(wv)
            ylast = None
            last_a = None
            for n in range(16):
                sa = 4 + (n & 1)
                sbb = 8 + (n & 1)
                sh = 6 + (n & 1)
                a1 = self.S((lambda e, q=q, n=n, sa=sa, sd=sd: e.activation(
                    sb["SCRF"][:, ts_(sa, 512)], sb["SCRF"][:, ts_(sd, 512)], ACT.Exp,
                    scale=sb["c_A"][:, q * 16 + n:q * 16 + n + 1])),
                    waits=[delta_tok, a_free[n & 1], self.ld["c_A"]], label="abuild")
                a_tok = self.need(a1)
                last_a = a_tok
                b1 = self.G((lambda e, n=n, q=q, sbb=sbb: e.tensor_tensor(
                    sb["SCRB"][:, ts_(sbb, 512)], sb["SCRB"][:, ts_(4 + (q & 1), 512)],
                    sb["A3"][:, ts_(n, 512)], ALU.mult)),
                    waits=[w_tok, brep_tok[n], b_free[n & 1]], label="bbuild")
                b_tok = self.need(b1)
                self.brep_free[n] = b_tok
                h1 = self.V((lambda e, n=n, q=q, sa=sa, sbb=sbb, sh=sh: e.tensor_tensor_scan(
                    sb["SCRB"][:, ts_(sh, 512)], sb["SCRF"][:, ts_(sa, 512)], sb["SCRB"][:, ts_(sbb, 512)],
                    sb["mstate"][:, q * 16 + n:q * 16 + n + 1], ALU.mult, ALU.add)),
                    waits=[a_tok, b_tok, h_free[n & 1], self.state_tok], label="scan")
                h_tok = self.need(h1)
                a_free[n & 1] = h_tok
                b_free[n & 1] = h_tok
                if n == 0:
                    ym = self.V((lambda e, n=n, sh=sh: e.tensor_tensor(
                        sb["SCRF"][:, ts_(6, 512)], sb["SCRB"][:, ts_(sh, 512)],
                        sb["A4"][:, ts_(n, 512)], ALU.mult)),
                        waits=[crep_tok[n], self.y_free], label="ymul0")
                    ylast = self.need(ym)
                    h_free[n & 1] = ylast
                    self.crep_free[n] = ylast
                else:
                    ym = self.V((lambda e, n=n, sh=sh: e.tensor_tensor(
                        sb["SCRB"][:, ts_(0 + (n & 1), 512)], sb["SCRB"][:, ts_(sh, 512)],
                        sb["A4"][:, ts_(n, 512)], ALU.mult)),
                        waits=[crep_tok[n], tmp_free[n & 1]], label="ymul")
                    ym = self.need(ym)
                    h_free[n & 1] = ym
                    self.crep_free[n] = ym
                    ya = self.V((lambda e, n=n: e.tensor_tensor(
                        sb["SCRF"][:, ts_(6 + (n & 1), 512)], sb["SCRF"][:, ts_(6 + ((n - 1) & 1), 512)],
                        sb["SCRB"][:, ts_(0 + (n & 1), 512)], ALU.add)), label="yacc")
                    ylast = self.need(ya)
                    tmp_free[n & 1] = ylast
                sc = self.V((lambda e, n=n, q=q, sh=sh: e.tensor_copy(
                    sb["mstate"][:, q * 16 + n:q * 16 + n + 1], sb["SCRB"][:, ts_(sh, 512)][:, 511:512])),
                    label="statecpy")
            self.delta_free[q & 1] = last_a
            self.w_free[q & 1] = h_tok
            self.u_free[q] = None
            g1 = self.V((lambda e, q=q, stmp=stmp: e.scalar_tensor_tensor(
                out=sb["SCRF"][:, ts_(stmp, 512)], in0=sb["A1"][:, ts_(q, 512)],
                scalar=sb["c_D"][:, q:q + 1], in1=sb["SCRF"][:, ts_(7, 512)],
                op0=ALU.mult, op1=ALU.add)),
                waits=[ylast, uready[q], self.ld["c_D"]], label="gate1")
            g1tok = self.need(g1)
            self.u_free[q] = g1tok
            g2 = self.V((lambda e, q=q, stmp=stmp: e.tensor_tensor(
                sb["yg"][:, ts_(q, 512)], sb["SCRF"][:, ts_(stmp, 512)],
                sb["A2"][:, ts_(q, 512)], ALU.mult)),
                waits=[g1tok, zready[q], self.yg_free], label="gate2")
            ygtok[q] = self.need(g2)
            self.z_free[q] = ygtok[q]
            self.scrt_free[q & 1] = ygtok[q]
            self.y_free = ygtok[q]
        self.state_tok = self.state_tok
        first = (c == 0)
        xrtok = [None] * 8
        for r in range(8):
            view, ready, release = self.ring.load("w_outproj", 0, 16, r, 1, first)
            bank = "p%d" % (r % 2)
            pairs = []
            for kk in range(16):
                pairs.append(((lambda v=view, kk=kk: v(kk, 0)),
                              (lambda kk=kk: sb["yg"][:, ts_(kk, 512)]),
                              [ygtok[kk]]))
            mm = self.mm_group(ps[bank][:, :], pairs,
                               waits=ready + [self.bank_free.get(bank)], label="outproj")
            mm = self.need(mm)
            release(mm)
            ev = self.V((lambda e, r=r, b=bank: e.tensor_tensor(
                sb["xres"][:, ts_(r, 512)], ps[b][:, :], sb["xres"][:, ts_(r, 512)], ALU.add)),
                waits=[mm], label="res_add")
            xrtok[r] = self.need(ev)
            self.bank_free[bank] = xrtok[r]
        self.yg_free = xrtok[-1]
        return xrtok

    def build(self):
        cfg = self.cfg
        self.declare()
        self.prog = Prog(self.nc, self.es)
        self.ring = WRing(self)
        self.sbuf("dtrep", (128, 512), BF16)
        self.sbuf("Asb", (128, 128), BF16)
        self.sbuf("Asb2", (128, 1280), BF16)
        self.sbuf("dtrow", (1, 512), BF16)
        self.sbuf("Bsb", (16, 512), BF16)
        self.sbuf("Csb", (16, 512), BF16)
        self._ccol_map = {}
        self._ccol_tok = {}
        self.bank_free = {}
        self.casts2_free = [None, None]
        self.delta_free = [None, None]
        self.w_free = [None, None]
        self.scrt_free = [None, None]
        self.y_free = None
        self.yg_free = None
        self.u_free = [None] * 16
        self.z_free = [None] * 16
        self.brep_free = [None] * 16
        self.crep_free = [None] * 16
        self.bsb_free = None
        self.tail_tok = [None] * 16
        self.a5_free = [None] * 8
        self.lam_free = [None, None]
        self.scrb_free = {i: None for i in range(8)}
        self.scrb89_free = [None, None]
        self.asb_free = None
        self.osq_free = None
        self.o1_free = None
        self.sg_free = None
        self.Sstate_tok = None
        self.colsc_free = []
        self.scrf0_free = None
        self.scrf67_free = None
        self.strow_free = None
        self._ln_dst16 = False
        self.prologue()
        self.Sstate_tok = [self.tok_S0] * 8
        self.state_tok = self.tok_state0
        for q in range(16):
            self.tail_tok[q] = self.tok_tails0
        blk = self.es.enter_context(self.nc.Block())
        out_toks = []
        prev_out = None
        for c in range(cfg.nch):
            xap = self.dram["x_fm"].rearrange("(r p) L -> p r L", p=128)
            xl = self.prog.dma("sync", (lambda e, c=c, xap=xap: e.dma_start(
                out=self.sb["xres"][:, :].rearrange("p (r t) -> p r t", r=8),
                in_=xap[:, :, c * cfg.Tc:(c + 1) * cfg.Tc])),
                waits=[prev_out if prev_out is not None else self.scrub_tok], label="xload", sem="dq_io")
            xtok = self.prog.need(xl)
            if 0 in cfg.blocks:
                ln_toks = self.ln(waits=[xtok])
                ur, zr = self.l0_attn(c, xtok, ln_toks)
                xr = self.l0_attn_rest(c, ur, zr)
            else:
                xr = [xtok] * 8
            if 1 in cfg.blocks:
                ln_toks = self.ln(waits=xr, lora_taps=(c == 0))
                xr = self.l0_ffn(c, ln_toks, xr)
            if 2 in cfg.blocks:
                ln_toks = self.ln(waits=xr)
                xr = self.l1_attn(c, ln_toks, xr)
            if 3 in cfg.blocks:
                self._ln_dst16 = True
                ln_toks = self.ln(waits=xr)
                self._ln_dst16 = False
                xr = self.l1_ffn(c, ln_toks, xr)
            oap = self.dram["out_fm"].rearrange("(r p) L -> p r L", p=128)
            so = self.prog.dma("sync", (lambda e, c=c, oap=oap: e.dma_start(
                out=oap[:, :, c * cfg.Tc:(c + 1) * cfg.Tc],
                in_=self.sb["xres"][:, :].rearrange("p (r t) -> p r t", r=8))),
                waits=list(xr), label="xstore", sem="dq_io")
            so = self.prog.need(so)
            out_toks.append(so)
            prev_out = so
        self.prog.op("gpsimd", lambda e: e.nop(), waits=out_toks, label="final_wait")
        self.prog.finalize(blk)
        self.es.close()
        return self.nc

    def l0_ffn(self, c, ln_toks, xr):
        sb = self.sb
        ps = self.ps
        first = (c == 0)
        cfg = self.cfg
        if first:
            cb = None
            for kk in range(8):
                cb = self.V((lambda e, kk=kk: e.tensor_copy(
                    sb["lora_b"][:, kk * 8:kk * 8 + 8],
                    sb["lora"][:, kk * 16 + 1:kk * 16 + 16:2])),
                    waits=self.lora_toks if kk == 0 else (), label="lora_cast")
            self.lora_b_tok = self.need(cb)
            prev = None
            for a in range(4):
                for r2 in range(4):
                    for rm in range(2):
                        dmaop = (lambda e, a=a, r2=r2, rm=rm: e.dma_start(
                            out=sb["e2r"][32 * a:32 * (a + 1), r2 * 8 + rm * 4:r2 * 8 + rm * 4 + 4],
                            in_=sb["lora"][r2:r2 + 125:4, (rm * 4 + a) * 16 + 8:(rm * 4 + a) * 16 + 15:2]))
                        def dmaop2(e, dmaop=dmaop):
                            with self.nc.allow_non_contiguous_dma(reason="tiny one-time gathers"):
                                return dmaop(e)
                        d = self.prog.dma("sync", dmaop2, waits=[prev] + self.lora_toks,
                                          label="e2r", sem="dq_pro")
                        prev = self.prog.need(d)
            self.e2r_tok = prev
        pairs = []
        for kk in range(8):
            pairs.append(((lambda kk=kk: sb["lora_b"][:, kk * 8:kk * 8 + 4]),
                          (lambda kk=kk: sb["xn"][:, ts_(kk, 512)]),
                          [ln_toks[kk], self.lora_b_tok]))
        mmp = self.need(self.mm_group(ps["p3"][0:4, :], pairs,
                                      waits=[self.bank_free.get("p3")], label="p_mm"))
        evp = self.need(self.V(lambda e: e.tensor_scalar(
            out=sb["Bsb"][0:4, :], in0=ps["p3"][0:4, :], scalar1=sb["c_scale"][0:4, :],
            scalar2=None, op0=ALU.mult), waits=[mmp, self.bsb_free, self.ld["c_scale"]]))
        self.bank_free["p3"] = evp
        prep_tok = [None] * 4
        for i in range(4):
            bank = "p%d" % (4 + i % 2)
            mm = self.need(self.T((lambda e, i=i, b=bank: e.matmul(
                ps[b][:, :], sb["c_ind4"][:, i * 128:(i + 1) * 128], sb["Bsb"][0:4, :],
                start=True, stop=True)),
                waits=[evp, self.bank_free.get(bank), self.ld["c_ind4"]], label="prep_bc"))
            ev = self.need(self.S((lambda e, i=i, b=bank: e.copy(sb["A4"][:, ts_(i, 512)], ps[b][:, :])),
                                  waits=[mm, self.crep_free[i]]))
            prep_tok[i] = ev
            self.bank_free[bank] = ev
        h1tok = [None] * 32
        view = None
        rel = None
        for m in range(32):
            if m % 2 == 0:
                view, ready, rel = self.ring.load("w_W1", 0, 8, m, 2, first)
            bank = "p%d" % (m % 2)
            pairs = []
            for kk in range(8):
                pairs.append(((lambda v=view, kk=kk, mi=m % 2: v(kk, mi)),
                              (lambda kk=kk: sb["xn"][:, ts_(kk, 512)]),
                              [ln_toks[kk]]))
            mm = self.need(self.mm_group(ps[bank][:, :], pairs,
                                         waits=ready + [self.bank_free.get(bank)], label="w1"))
            i = m // 8
            lor = self.need(self.V((lambda e, m=m, i=i, b=bank: e.scalar_tensor_tensor(
                out=ps[b][:, :], in0=sb["A4"][:, ts_(i, 512)],
                scalar=sb["lora"][:, (m % 8) * 16 + 2 * i:(m % 8) * 16 + 2 * i + 1],
                in1=ps[b][:, :], op0=ALU.mult, op1=ALU.add)),
                waits=[mm, prep_tok[i]], label="w1lora"))
            guard = self.u_free[m] if m < 16 else self.z_free[m - 16]
            dst = (lambda e, m=m, b=bank: e.activation(
                sb["A1" if m < 16 else "A2"][:, ts_(m if m < 16 else m - 16, 512)],
                ps[b][:, :], ACT.Gelu, bias=sb["c_b1"][:, m:m + 1]))
            ev = self.need(self.S(dst, waits=[lor, guard, self.ld["c_b1"]], label="gelu"))
            if m % 2 == 1:
                rel(mm)
            h1tok[m] = ev
            self.bank_free[bank] = ev
        if cfg.debug_taps and c == 0:
            t0 = self.prog.dma("sync", lambda e: e.dma_start(
                out=self.dram["tap_lora"], in_=sb["lora"][:, :]),
                waits=self.lora_toks, label="tap", sem="dq_pro")
            t0b = self.prog.dma("sync", lambda e: e.dma_start(
                out=self.dram["tap_prow"][0:4, :], in_=sb["Bsb"][0:4, :]),
                waits=[evp, t0], label="tap", sem="dq_pro")
            self.prog.need(t0b)
            t1 = self.prog.dma("sync", lambda e: e.dma_start(
                out=self.dram["tap_h1"][:, 0:8192], in_=sb["A1"][:, :]),
                waits=h1tok + [self.prog.need(t0b)], label="tap", sem="dq_pro")
            t2 = self.prog.dma("sync", lambda e: e.dma_start(
                out=self.dram["tap_h1"][:, 8192:16384], in_=sb["A2"][:, :]),
                waits=h1tok + [t1], label="tap", sem="dq_pro")
            t3 = self.prog.dma("sync", lambda e: e.dma_start(
                out=self.dram["tap_prep"], in_=sb["A4"][:, 0:2048]),
                waits=prep_tok + [t2], label="tap", sem="dq_pro")
            self.prog.need(t3)
            self.tap_toks = [self.prog.need(t1), self.prog.need(t2), self.prog.need(t3)]
        grep_tok = [None] * 16
        gprev = None
        for r2 in range(4):
            pairs = []
            for kk in range(8):
                m = 8 * r2 + kk
                pairs.append(((lambda kk=kk: sb["lora_b"][:, kk * 8 + 4:kk * 8 + 8]),
                              (lambda m=m: sb["A1" if m < 16 else "A2"][:, ts_(m if m < 16 else m - 16, 512)]),
                              [h1tok[m]]))
            mmg = self.need(self.mm_group(ps["p3"][0:4, :], pairs,
                                          waits=[self.bank_free.get("p3")], label="g_mm"))
            evg = self.need(self.V((lambda e: e.tensor_scalar(
                out=sb["Bsb"][0:4, :], in0=ps["p3"][0:4, :], scalar1=sb["c_scale"][0:4, :],
                scalar2=None, op0=ALU.mult)), waits=[mmg, gprev]))
            self.bank_free["p3"] = evg
            for i in range(4):
                np_ = r2 * 4 + i
                bank = "p%d" % (4 + i % 2)
                mm = self.need(self.T((lambda e, i=i, b=bank: e.matmul(
                    ps[b][:, :], sb["c_ind4"][:, i * 128:(i + 1) * 128], sb["Bsb"][0:4, :],
                    start=True, stop=True)),
                    waits=[evg, self.bank_free.get(bank)], label="grep_bc"))
                gprev = mm
                ev = self.need(self.S((lambda e, np_=np_, b=bank: e.copy(
                    sb["A3"][:, ts_(np_, 512)], ps[b][:, :])),
                    waits=[mm, self.brep_free[np_]]))
                grep_tok[np_] = ev
                self.bank_free[bank] = ev
        self.bsb_free = gprev
        if cfg.debug_taps and c == 0:
            t4 = self.prog.dma("sync", lambda e: e.dma_start(
                out=self.dram["tap_grep"], in_=sb["A3"][:, :]),
                waits=grep_tok + self.tap_toks, label="tap", sem="dq_pro")
            self.tap_toks.append(self.prog.need(t4))
        xrtok = [None] * 8
        for r in range(8):
            views = []
            readies = []
            rels = []
            for g in range(2):
                v, rd, rl = self.ring.load("w_W2", g * 2048, 16, r, 1, first)
                views.append(v)
                readies.append(rd)
                rels.append(rl)
            bank = "p%d" % (r % 2)
            pairs = []
            for kk in range(32):
                m = kk
                pairs.append(((lambda kk=kk, views=views: views[kk // 16](kk % 16, 0)),
                              (lambda m=m: sb["A1" if m < 16 else "A2"][:, ts_(m if m < 16 else m - 16, 512)]),
                              [h1tok[m]] + (readies[kk // 16] if kk % 16 == 0 else [])))
            mm = self.need(self.mm_group(ps[bank][:, :], pairs,
                                         waits=[self.bank_free.get(bank)], label="w2"))
            for g in range(2):
                rels[g](mm)
            i = r // 2
            last = mm
            for r2 in range(4):
                np_ = r2 * 4 + i
                ecol = r2 * 8 + (r % 2) * 4 + (r // 2)
                last = self.need(self.V((lambda e, r=r, r2=r2, np_=np_, ecol=ecol, b=bank:
                                         e.scalar_tensor_tensor(
                    out=ps[b][:, :], in0=sb["A3"][:, ts_(np_, 512)],
                    scalar=sb["e2r"][:, ecol:ecol + 1], in1=ps[b][:, :],
                    op0=ALU.mult, op1=ALU.add)),
                    waits=[last, grep_tok[np_], self.e2r_tok], label="w2lora"))
            bb = self.need(self.V((lambda e, r=r, b=bank: e.tensor_scalar(
                out=ps[b][:, :], in0=ps[b][:, :], scalar1=sb["c_b2"][:, r:r + 1], scalar2=None,
                op0=ALU.add)), waits=[last, self.ld["c_b2"]], label="b2"))
            ev = self.need(self.V((lambda e, r=r, b=bank: e.tensor_tensor(
                sb["xres"][:, ts_(r, 512)], ps[b][:, :], sb["xres"][:, ts_(r, 512)], ALU.add)),
                waits=[bb, xr[r]], label="res_add2"))
            xrtok[r] = ev
            self.bank_free[bank] = ev
        for q in range(16):
            self.u_free[q] = xrtok[-1]
            self.z_free[q] = xrtok[-1]
        for n in range(16):
            self.brep_free[n] = xrtok[-1]
            self.crep_free[n] = xrtok[-1]
        return xrtok

    def l1_attn(self, c, ln_toks, xr):
        sb = self.sb
        ps = self.ps
        first = (c == 0)
        cfg = self.cfg
        c16, c16t = self.ccol(1.0 / 16.0)
        cm16, cm16t = self.ccol(-1.0 / 16.0)
        c512i, c512it = self.ccol(1.0 / 512.0)
        ceps, cepst = self.ccol(1e-5)
        cnh, cnht = self.ccol(-0.5)
        consts = [c16t, cm16t, c512it, cepst, cnht]
        qk_tok = [None] * 16
        for j, mat in enumerate(["w_Wq", "w_Wk"]):
            for g2 in range(4):
                view, ready, rel = self.ring.load(mat, 0, 8, g2 * 2, 2, first)
                for mi in range(2):
                    m = g2 * 2 + mi
                    bank = "p%d" % (m % 2)
                    pairs = []
                    for kk in range(8):
                        pairs.append(((lambda v=view, kk=kk, mi=mi: v(kk, mi)),
                                      (lambda kk=kk: sb["xn"][:, ts_(kk, 512)]),
                                      [ln_toks[kk]]))
                    mm = self.need(self.mm_group(ps[bank][:, :], pairs,
                                                 waits=ready + [self.bank_free.get(bank)], label=mat))
                    if mi == 1:
                        rel(mm)
                    n_ = j * 8 + m
                    if j == 0:
                        ev = self.need(self.S((lambda e, n_=n_, b=bank: e.activation(
                            sb["A4"][:, ts_(n_, 512)], ps[b][:, :], ACT.Copy, scale=0.0625)),
                            waits=[mm, self.crep_free[n_]], label="qevac"))
                    else:
                        ev = self.need(self.S((lambda e, n_=n_, b=bank: e.copy(
                            sb["A4"][:, ts_(n_, 512)], ps[b][:, :])),
                            waits=[mm, self.crep_free[n_]], label="kevac"))
                    qk_tok[n_] = ev
                    self.bank_free[bank] = ev
        for b in ("p0", "p1"):
            for half in (0, 1):
                if (b, half) not in self.bank_free:
                    self.bank_free[(b, half)] = self.bank_free.get(b)
        vg_tok = {}
        for j, mat in enumerate(["w_Wv", "w_Wg"]):
            arena = "A1" if j == 0 else "A2"
            guards = self.u_free if j == 0 else self.z_free
            for ee2 in range(8):
                view, ready, rel = self.ring.load(mat, 0, 8, ee2 * 2, 2, first)
                mm = None
                for tt in range(4):
                    bank = "p%d" % (tt % 2)
                    half = ee2 % 2
                    pairs = []
                    for kk in range(8):
                        pairs.append((
                            (lambda kk=kk, tt=tt: sb["xn"][:, kk * 512 + tt * 128:kk * 512 + (tt + 1) * 128]),
                            (lambda v=view, kk=kk: v(kk, 0) if False else None),
                            []))
                    pairs = []
                    for kk in range(8):
                        pairs.append((
                            (lambda kk=kk, tt=tt: sb["xn"][:, kk * 512 + tt * 128:kk * 512 + (tt + 1) * 128]),
                            (lambda v=view, kk=kk: v(kk, 0).tensor[:, (kk * 2) * 128:(kk * 2 + 2) * 128]),
                            [ln_toks[kk]]))
                    mm = self.need(self.mm_group(
                        ps[bank][:, ts_(half, 256)], pairs,
                        waits=ready + [self.bank_free.get((bank, half))], label=mat))
                    tile = tt * 4 + ee2 // 2
                    ev = self.need(self.S((lambda e, b=bank, half=half, arena=arena, tile=tile, tt=tt, ee2=ee2:
                                           e.copy(sb[arena][:, tt * 2048 + (ee2 // 2) * 512 + half * 256:
                                                            tt * 2048 + (ee2 // 2) * 512 + half * 256 + 256],
                                                  ps[b][:, ts_(half, 256)])),
                        waits=[mm, guards[tile]], label="vgevac"))
                    vg_tok[(j, tt, ee2)] = ev
                    self.bank_free[(bank, half)] = ev
                rel(mm)
        for b in ("p0", "p1"):
            toks = [self.bank_free.get((b, 0)), self.bank_free.get((b, 1))]
            self.bank_free[b] = [t for t in toks if t is not None] or None
        pairs = []
        for kk in range(8):
            pairs.append(((lambda kk=kk: sb["c_wgk1"][:, kk * 16:(kk + 1) * 16]),
                          (lambda kk=kk: sb["xn"][:, ts_(kk, 512)]),
                          [ln_toks[kk], self.ld["c_wgk1"]]))
        mmz1 = self.need(self.mm_group(ps["p3"][0:16, :], pairs,
                                       waits=[self.bank_free.get("p3")], label="gk1"))
        evz1 = self.need(self.S(lambda e: e.copy(sb["Bsb"][:, :], ps["p3"][0:16, :]),
                                waits=[mmz1, self.bsb_free]))
        self.bank_free["p3"] = evz1
        gk_tok = [None] * 8
        for r in range(8):
            bank = "p%d" % (2 + r % 2)
            mm = self.need(self.T((lambda e, r=r, b=bank: e.matmul(
                ps[b][:, :], sb["c_wgk2"][:, r * 128:(r + 1) * 128], sb["Bsb"][:, :],
                start=True, stop=True)),
                waits=[evz1, self.bank_free.get(bank), self.ld["c_wgk2"]], label="gk2"))
            s0 = 0 + r % 2
            s1 = 2 + r % 2
            e1 = self.V((lambda e, r=r, b=bank, s0=s0: e.tensor_scalar(
                out=sb["SCRF"][:, ts_(s0, 512)], in0=ps[b][:, :],
                scalar1=sb["c_bgk2"][:, r:r + 1], scalar2=None, op0=ALU.add)),
                waits=[mm, self.ld["c_bgk2"], self.scrt_free[r % 2]], label="gkbias")
            e1tok = self.need(e1)
            self.bank_free[bank] = e1tok
            e2 = self.S((lambda e, s0=s0, s1=s1: e.activation(
                sb["SCRF"][:, ts_(s1, 512)], sb["SCRF"][:, ts_(s0, 512)], ACT.Abs)), waits=[e1tok])
            e3 = self.S((lambda e, s1=s1: e.activation(
                sb["SCRF"][:, ts_(s1, 512)], sb["SCRF"][:, ts_(s1, 512)], ACT.Exp, scale=-1.0)))
            e4 = self.S((lambda e, s1=s1: e.activation(
                sb["SCRF"][:, ts_(s1, 512)], sb["SCRF"][:, ts_(s1, 512)], ACT.Ln, bias=1.0)))
            e5 = self.V((lambda e, r=r, s0=s0, s1=s1: e.scalar_tensor_tensor(
                out=sb["A5"][:, ts_(r, 512)], in0=sb["SCRF"][:, ts_(s0, 512)], scalar=0.0,
                in1=sb["SCRF"][:, ts_(s1, 512)], op0=ALU.min, op1=ALU.subtract)),
                waits=[self.need(e4), self.a5_free[r]], label="gk16")
            gk_tok[r] = self.need(e5)
            self.scrt_free[r % 2] = gk_tok[r]
        ostate_tok = self.Sstate_tok
        ofm_tok = [None] * 16
        k2_free = [None, None]
        osl_free = [None] * 3
        for sc in range(4):
            for h in range(HEADS):
                cols_toks = []
                for pi in range(2):
                    r = 2 * h + pi
                    t1 = self.V((lambda e, r=r, sc=sc, pi=pi: e.tensor_reduce(
                        sb["colsc"][:, pi:pi + 1],
                        sb["A5"][:, r * 512 + sc * 128:r * 512 + sc * 128 + 64], AX.X, ALU.add)),
                        waits=[gk_tok[r]] + self.colsc_free, label="mcol")
                    self.colsc_free = []
                    t2 = self.V((lambda e, r=r, sc=sc, pi=pi: e.tensor_reduce(
                        sb["colsc"][:, 2 + pi:3 + pi],
                        sb["A5"][:, r * 512 + sc * 128:r * 512 + (sc + 1) * 128], AX.X, ALU.add)),
                        label="ccol")
                for pi in range(2):
                    t3 = self.V((lambda e, pi=pi: e.tensor_scalar_mul(
                        sb["colsc"][:, 4 + pi:5 + pi], sb["colsc"][:, pi:pi + 1], 1.0 / 16.0)))
                    t4 = self.V((lambda e, pi=pi: e.tensor_scalar_mul(
                        sb["colsc"][:, 6 + pi:7 + pi], sb["colsc"][:, pi:pi + 1], -1.0 / 16.0)))
                    t5 = self.V((lambda e, pi=pi: e.tensor_scalar_mul(
                        sb["colsc"][:, 8 + pi:9 + pi], sb["colsc"][:, 2 + pi:3 + pi], 1.0 / 16.0)))
                    cols_toks.append(self.need(t5))
                lam = [None, None]
                lam_toks = [None, None]
                for pi in range(2):
                    r = 2 * h + pi
                    sl = 4 + pi
                    lam[pi] = sl
                    sc_ap = (lambda e, r=r, sc=sc, sl=sl: e.tensor_tensor_scan(
                        sb["SCRF"][:, sl * 512:sl * 512 + 128],
                        sb["A5"][:, r * 512 + sc * 128:r * 512 + (sc + 1) * 128],
                        sb["A5"][:, r * 512 + sc * 128:r * 512 + (sc + 1) * 128],
                        0.0, ALU.add, ALU.bypass))
                    lam_toks[pi] = self.need(self.V(sc_ap, waits=[gk_tok[r], self.lam_free[pi]],
                                                    label="cumsum"))
                e_toks = {}
                for pi in range(2):
                    specs = [("Ep", c16, 6 + pi, 0 + pi * 4), ("Em", cm16, 4 + pi, 1 + pi * 4),
                             ("Ef", c16, None, 2 + pi * 4), ("Epp", cm16, 8 + pi, 3 + pi * 4)]
                    for nm, scl, bcol, dst in specs:
                        bias = 0.0 if bcol is None else None
                        fn = (lambda e, pi=pi, scl=scl, bcol=bcol, dst=dst, lamsl=lam[pi]:
                              e.activation(sb["SCRB"][:, dst * 512:dst * 512 + 128],
                                           sb["SCRF"][:, lamsl * 512:lamsl * 512 + 128], ACT.Exp,
                                           scale=scl,
                                           bias=(0.0 if bcol is None else sb["colsc"][:, bcol:bcol + 1])))
                        e_toks[(pi, nm)] = self.need(self.S(fn,
                            waits=[lam_toks[pi]] + cols_toks + consts + [self.scrb_free[dst]],
                            label="glaexp"))
                    e_toks[(pi, "ec2")] = self.need(self.S((lambda e, pi=pi: e.activation(
                        sb["colsc"][:, 12 + pi:13 + pi], sb["colsc"][:, 2 + pi:3 + pi], ACT.Exp,
                        scale=c16)), waits=cols_toks + consts, label="ec2"))
                var_toks = {}
                for pi in range(2):
                    r = 2 * h + pi
                    qsl = (lambda off, pi=pi: sb["SCRB"][:, (8 + pi) * 512 + off * 128:(8 + pi) * 512 + (off + 1) * 128])
                    for off, (nm, src_n, esl) in enumerate([("qt", r, 0), ("kt", 8 + r, 1),
                                                            ("q2", r, 2), ("k2", 8 + r, 3)]):
                        fn = (lambda e, pi=pi, off=off, src_n=src_n, esl=esl, sc=sc:
                              e.tensor_tensor(
                            sb["SCRB"][:, (8 + pi) * 512 + off * 128:(8 + pi) * 512 + (off + 1) * 128],
                            sb["A4"][:, src_n * 512 + sc * 128:src_n * 512 + (sc + 1) * 128],
                            sb["SCRB"][:, (esl + pi * 4) * 512 // 4 * 4:(esl + pi * 4) * 512 // 4 * 4 + 128]
                            if False else sb["SCRB"][:, (esl + pi * 4) * 512:(esl + pi * 4) * 512 + 128],
                            ALU.mult))
                        var_toks[(pi, nm)] = self.need(self.V(fn,
                            waits=[e_toks[(pi, ["Ep", "Em", "Ef", "Epp"][esl])], qk_tok[src_n],
                                   self.scrb89_free[pi]], label="qkvar"))
                mmA = self.need(self.mm_group(ps["p2"][:, 0:128],
                    [((lambda pi=pi: sb["SCRB"][:, (8 + pi) * 512 + 1 * 128:(8 + pi) * 512 + 2 * 128]),
                      (lambda pi=pi: sb["SCRB"][:, (8 + pi) * 512 + 0 * 128:(8 + pi) * 512 + 1 * 128]),
                      [var_toks[(pi, "kt")], var_toks[(pi, "qt")]]) for pi in range(2)],
                    waits=[self.bank_free.get("p2")], label="glaA"))
                mask = self.need(self.V(lambda e: e.tensor_tensor(
                    sb["Asb"][:, :], ps["p2"][:, 0:128], sb["c_mask"][:, :], ALU.mult),
                    waits=[mmA, self.ld["c_mask"], self.asb_free], label="glamask"))
                self.bank_free["p2"] = mask
                vslice = (lambda e=None, sc=sc, h=h: sb["A1"][:, sc * 2048 + h * 512:sc * 2048 + (h + 1) * 512])
                mmO = self.mm_group(ps["p6"][:, :],
                    [((lambda: sb["Asb"][:, :]), vslice, [mask, vg_tok[(0, sc, h * 2)], vg_tok[(0, sc, h * 2 + 1)]])] +
                    [((lambda pi=pi: sb["SCRB"][:, (8 + pi) * 512 + 2 * 128:(8 + pi) * 512 + 3 * 128]),
                      (lambda pi=pi, h=h: sb["Sstate"][:, (2 * h + pi) * 512:(2 * h + pi + 1) * 512]),
                      [var_toks[(pi, "q2")], ostate_tok[2 * h + pi]]) for pi in range(2)],
                    waits=[self.bank_free.get("p6")], label="glaO")
                mmO = self.need(mmO)
                supd_toks = []
                for pi in range(2):
                    trk = self.need(self.T((lambda e, pi=pi: e.transpose(
                        ps["p7"][:, 0:128],
                        sb["SCRB"][:, (8 + pi) * 512 + 3 * 128:(8 + pi) * 512 + 4 * 128],
                        sb["c_ident"][:, :])),
                        waits=[var_toks[(pi, "k2")], self.bank_free.get("p7"), self.ld["c_ident"]],
                        label="k2tr"))
                    evk = self.need(self.S((lambda e, pi=pi: e.copy(
                        sb["Asb2"][:, pi * 128:(pi + 1) * 128], ps["p7"][:, 0:128])),
                        waits=[trk, k2_free[pi]], label="k2ev"))
                    self.bank_free["p7"] = evk
                    bank = "p%d" % (4 + pi)
                    mmS = self.need(self.T((lambda e, pi=pi, b=bank, vslice=vslice: e.matmul(
                        ps[b][:, :], sb["Asb2"][:, pi * 128:(pi + 1) * 128], vslice(),
                        start=True, stop=True)),
                        waits=[evk, self.bank_free.get(bank), vg_tok[(0, sc, h * 2)], vg_tok[(0, sc, h * 2 + 1)]],
                        label="dS"))
                    su = self.need(self.V((lambda e, pi=pi, h=h, b=bank: e.scalar_tensor_tensor(
                        out=sb["Sst32"][:, (2 * h + pi) * 512:(2 * h + pi + 1) * 512],
                        in0=sb["Sst32"][:, (2 * h + pi) * 512:(2 * h + pi + 1) * 512],
                        scalar=sb["colsc"][:, 12 + pi:13 + pi],
                        in1=ps[b][:, :], op0=ALU.mult, op1=ALU.add)),
                        waits=[mmS, e_toks[(pi, "ec2")], mmO], label="Supd"))
                    su = self.need(self.V((lambda e, pi=pi, h=h: e.tensor_copy(
                        sb["Sstate"][:, (2 * h + pi) * 512:(2 * h + pi + 1) * 512],
                        sb["Sst32"][:, (2 * h + pi) * 512:(2 * h + pi + 1) * 512])),
                        waits=[su], label="Scpy"))
                    k2_free[pi] = su
                    ostate_tok[2 * h + pi] = su
                    self.bank_free[bank] = su
                    supd_toks.append(su)
                sq = self.need(self.S((lambda e: e.activation(
                    sb["Asb2"][:, 256:256 + 512], ps["p6"][:, :], ACT.Square,
                    accum_out=sb["colsc"][:, 14:15])), waits=[mmO, self.osq_free], label="osq"))
                sg = self.need(self.S((lambda e, sc=sc, h=h: e.activation(
                    sb["Asb2"][:, 768:1280],
                    sb["A2"][:, sc * 2048 + h * 512:sc * 2048 + (h + 1) * 512], ACT.Silu)),
                    waits=[vg_tok[(1, sc, h * 2)], vg_tok[(1, sc, h * 2 + 1)], self.sg_free], label="sg"))
                r1 = self.S((lambda e: e.activation(sb["colsc"][:, 14:15], sb["colsc"][:, 14:15],
                                                    ACT.Ln, scale=c512i, bias=ceps)),
                            waits=[sq] + consts)
                spc = self.S((lambda e, sc=sc, h=h: e.copy(
                    sb["Asb2"][:, 256:256 + 256],
                    sb["A2"][:, sc * 2048 + h * 512:sc * 2048 + h * 512 + 256])),
                    label="rstd_spacer")
                r2 = self.need(self.S((lambda e: e.activation(
                    sb["colsc"][:, 14:15], sb["colsc"][:, 14:15], ACT.Exp, scale=cnh)),
                    waits=[self.need(r1)] + consts, label="rstd"))
                o1 = self.need(self.V((lambda e: e.tensor_scalar(
                    out=sb["Asb2"][:, 256:768], in0=ps["p6"][:, :],
                    scalar1=sb["colsc"][:, 14:15], scalar2=None, op0=ALU.mult)),
                    waits=[r2, mmO, self.o1_free], label="o1"))
                self.bank_free["p6"] = o1
                g1 = self.V((lambda e: e.tensor_tensor(
                    sb["Asb2"][:, 256:768], sb["Asb2"][:, 256:768],
                    sb["c_gnw"][:, :], ALU.mult)), waits=[o1, self.ld["c_gnw"]], label="ognw")
                g2 = self.need(self.V((lambda e: e.tensor_tensor(
                    sb["Asb2"][:, 256:768], sb["Asb2"][:, 256:768],
                    sb["Asb2"][:, 768:1280], ALU.mult)), waits=[sg], label="osil"))
                self.sg_free = g2
                self.osq_free = g2
                self.o1_free = None
                prevtr = g2
                for jb in range(4):
                    trt = self.need(self.T((lambda e, jb=jb: e.transpose(
                        ps["p7"][:, 0:128], sb["Asb2"][:, 256 + jb * 128:256 + (jb + 1) * 128],
                        sb["c_ident"][:, :])),
                        waits=[g2, self.bank_free.get("p7")], label="otr"))
                    et = h * 4 + jb
                    evo = self.need(self.S((lambda e, et=et, sc=sc: e.copy(
                        sb["A3"][:, et * 512 + sc * 128:et * 512 + (sc + 1) * 128], ps["p7"][:, 0:128])),
                        waits=[trt, (self.brep_free[et] if sc == 0 else None)], label="oev"))
                    self.bank_free["p7"] = evo
                    ofm_tok[et] = evo
                    prevtr = trt
                self.o1_free = prevtr
                self.lam_free = [e_toks[(0, 'Epp')], e_toks[(1, 'Epp')]]
                for pi in range(2):
                    self.scrb89_free[pi] = mmA if False else supd_toks[pi]
                for dst in range(8):
                    self.scrb_free[dst] = var_toks[(dst // 4, ["qt", "kt", "q2", "k2"][dst % 4])]
                self.asb_free = mmO
                self.colsc_free = [e_toks[(0, "ec2")], e_toks[(1, "ec2")], r2]
        self.Sstate_tok = ostate_tok
        if cfg.debug_taps and c == 0:
            tq = self.prog.dma("sync", lambda e: e.dma_start(out=self.dram["tap_qk"], in_=sb["A4"][:, :]),
                               waits=qk_tok, label="tap", sem="dq_pro")
            tgk = self.prog.dma("sync", lambda e: e.dma_start(out=self.dram["tap_gk"], in_=sb["A5"][:, :]),
                                waits=gk_tok + [self.prog.need(tq)], label="tap", sem="dq_pro")
            to = self.prog.dma("sync", lambda e: e.dma_start(out=self.dram["tap_ofm"], in_=sb["A3"][:, :]),
                               waits=ofm_tok + [self.prog.need(tgk)], label="tap", sem="dq_pro")
            self.prog.need(to)
        xrtok = [None] * 8
        for r in range(8):
            view, ready, rel = self.ring.load("w_Wo", 0, 16, r, 1, first)
            bank = "p%d" % (r % 2)
            pairs = []
            for kk in range(16):
                pairs.append(((lambda v=view, kk=kk: v(kk, 0)),
                              (lambda kk=kk: sb["A3"][:, ts_(kk, 512)]),
                              [ofm_tok[kk]]))
            mm = self.need(self.mm_group(ps[bank][:, :], pairs,
                                         waits=ready + [self.bank_free.get(bank)], label="wo"))
            rel(mm)
            ev = self.need(self.V((lambda e, r=r, b=bank: e.tensor_tensor(
                sb["xres"][:, ts_(r, 512)], ps[b][:, :], sb["xres"][:, ts_(r, 512)], ALU.add)),
                waits=[mm, xr[r]], label="res_add3"))
            xrtok[r] = ev
            self.bank_free[bank] = ev
        for n in range(16):
            self.crep_free[n] = xrtok[-1]
            self.brep_free[n] = xrtok[-1]
        for q in range(16):
            self.u_free[q] = xrtok[-1]
            self.z_free[q] = xrtok[-1]
        for r in range(8):
            self.a5_free[r] = xrtok[-1]
        return xrtok

    def l1_ffn(self, c, ln_toks, xr):
        sb = self.sb
        ps = self.ps
        first = (c == 0)

        def act_quant(src_fn, ntiles, src_toks, mx_bank, rc_bank, xi_dst_fn, xi_guards, lbl):
            acc = None
            for i in range(ntiles):
                sl = 2 + i % 2
                ab = self.S((lambda e, i=i, sl=sl: e.activation(
                    sb["SCRF"][:, ts_(sl, 512)], src_fn(i), ACT.Abs)),
                    waits=[src_toks[i], self.scrt_free[i % 2]], label=lbl + "abs")
                ab = self.need(ab)
                self.scrt_free[i % 2] = ab
                if i == 0:
                    acc = self.need(self.V((lambda e, sl=sl: e.tensor_copy(
                        sb["SCRF"][:, ts_(0, 512)], sb["SCRF"][:, ts_(sl, 512)])),
                        waits=[ab, self.scrf0_free], label=lbl + "cp"))
                else:
                    acc = self.need(self.V((lambda e, sl=sl: e.tensor_tensor(
                        sb["SCRF"][:, ts_(0, 512)], sb["SCRF"][:, ts_(0, 512)],
                        sb["SCRF"][:, ts_(sl, 512)], ALU.max)), waits=[ab], label=lbl + "max"))
                self.scrt_free[i % 2] = acc
            colt = []
            for j in range(4):
                tr = self.need(self.T((lambda e, j=j: e.transpose(
                    ps["p6"][:, 0:128], sb["SCRF"][:, j * 128:(j + 1) * 128],
                    sb["c_ident_f"][:, :])),
                    waits=[acc, self.bank_free.get("p6"), self.ld["c_ident_f"]], label=lbl + "tr"))
                rd = self.need(self.V((lambda e, j=j: e.tensor_reduce(
                    sb["colsc"][:, j:j + 1], ps["p6"][:, 0:128], AX.X, ALU.max)),
                    waits=[tr] + self.colsc_free, label=lbl + "red"))
                self.colsc_free = []
                self.bank_free["p6"] = rd
                colt.append(rd)
            srt = []
            for j in range(4):
                tr = self.need(self.T((lambda e, j=j: e.transpose(
                    ps["p6"][0:1, 0:128], sb["colsc"][:, j:j + 1], sb["c_ident_f"][:, :])),
                    waits=[colt[j], self.bank_free.get("p6")], label=lbl + "tr2"))
                ev = self.need(self.S((lambda e, j=j: e.copy(
                    sb["strow"][0:1, j * 128:(j + 1) * 128], ps["p6"][0:1, 0:128])),
                    waits=[tr, self.strow_free], label=lbl + "srow"))
                self.strow_free = ev
                self.bank_free["p6"] = ev
                srt.append(ev)
            t1a = self.V(lambda e: e.tensor_scalar_max(
                sb["strow"][0:1, 512:1024], sb["strow"][0:1, 0:512], 1e-5), waits=srt, label=lbl + "clip")
            spc = self.V(lambda e: e.memset(sb["strow_b"][0:1, :], 0.0), label=lbl + "spc")
            mmx0 = self.need(self.T((lambda e, b=mx_bank: e.matmul(
                ps[b][:, :], sb["c_ones1f"][:, :], sb["strow"][0:1, 512:1024],
                start=True, stop=True)),
                waits=[self.need(t1a), self.bank_free.get(mx_bank), self.ld["c_ones1f"]], label=lbl + "bcx"))
            mmx = self.need(self.V((lambda e, b=mx_bank: e.tensor_copy(
                sb["SCRF"][:, ts_(6, 512)], ps[b][:, :])), waits=[mmx0, self.scrf67_free], label=lbl + "evx"))
            mrc = self.need(self.V((lambda e, b=mx_bank: e.reciprocal(
                sb["SCRF"][:, ts_(7, 512)], ps[b][:, :])), waits=[mmx0], label=lbl + "evr"))
            self.bank_free[mx_bank] = mrc
            xit = []
            for i in range(ntiles):
                sl = 2 + i % 2
                a1 = self.V((lambda e, i=i, sl=sl: e.tensor_tensor(
                    sb["SCRF"][:, ts_(sl, 512)], src_fn(i), sb["SCRF"][:, ts_(7, 512)], ALU.mult)),
                    waits=[mrc, self.scrt_free[i % 2]], label=lbl + "xs")
                a2 = self.V((lambda e, sl=sl: e.tensor_scalar(
                    out=sb["SCRF"][:, ts_(sl, 512)], in0=sb["SCRF"][:, ts_(sl, 512)],
                    scalar1=127.0, scalar2=MAGIC, op0=ALU.mult, op1=ALU.add)))
                a3 = self.V((lambda e, sl=sl: e.tensor_scalar(
                    out=sb["SCRF"][:, ts_(sl, 512)], in0=sb["SCRF"][:, ts_(sl, 512)],
                    scalar1=-MAGIC, scalar2=None, op0=ALU.add)))
                a4 = self.V((lambda e, i=i, sl=sl: e.tensor_scalar(
                    out=xi_dst_fn(i), in0=sb["SCRF"][:, ts_(sl, 512)],
                    scalar1=127.0, scalar2=-128.0, op0=ALU.min, op1=ALU.max)),
                    waits=[xi_guards[i]], label=lbl + "xi")
                a4 = self.need(a4)
                self.scrt_free[i % 2] = a4
                xit.append(a4)
            self.scrf0_free = xit[-1]
            return xit, mmx, mrc

        xit, mmx1, mrc1 = act_quant(
            (lambda i: sb["A5"][:, ts_(i, 512)]), 8, ln_toks, "p4", "p5",
            (lambda i: sb["yg"][:, ts_(i, 512)]), [self.yg_free] * 8, "aq1")
        if self.cfg.debug_taps and c == 0:
            tx = self.prog.dma("sync", lambda e: e.dma_start(out=self.dram["tap_xi"], in_=sb["yg"][:, 0:4096]),
                               waits=xit, label="tap", sem="dq_pro")
            ts2 = self.prog.dma("sync", lambda e: e.dma_start(out=self.dram["tap_sw"], in_=sb["swcol"][:, :]),
                                waits=[self.prog.need(tx)] + list(self.tok_bq.values()), label="tap", sem="dq_pro")
            self.prog.need(ts2)
        gutok = [None] * 32
        for m in range(32):
            views = {}
            readies = {}
            if m % 2 == 0:
                for j, mat in enumerate(["w_gate", "w_up"]):
                    v, rd, rl = self.ring.load(mat, 0, 8, m, 2, first, quant=3 * j + 1,
                                               extra_waits=[self.tok_bq[mat]] if first else [])
                    views[j] = (v, rl)
                    readies[j] = rd
                self._guviews = views
                self._gureadies = readies
            views = self._guviews
            readies = self._gureadies
            mm_j = {}
            for j in range(2):
                bank = "p%d" % j
                pairs = []
                for kk in range(8):
                    pairs.append(((lambda v=views[j][0], kk=kk, mi=m % 2: v(kk, mi)),
                                  (lambda kk=kk: sb["yg"][:, ts_(kk, 512)]),
                                  [xit[kk]] + (readies[j] if kk == 0 and m % 2 == 0 else [])))
                mm_j[j] = self.need(self.mm_group(ps[bank][:, :], pairs,
                                                  waits=[self.bank_free.get(bank)], label="guMM"))
                if m % 2 == 1:
                    views[j][1](mm_j[j])
            sl = 2 + m % 2
            d1 = self.need(self.V((lambda e, sl=sl: e.tensor_tensor(
                sb["SCRF"][:, ts_(sl, 512)], ps["p0"][:, :], sb["SCRF"][:, ts_(6, 512)], ALU.mult)),
                waits=[mm_j[0], mmx1, self.scrt_free[m % 2]], label="gdeq"))
            sgsl = m % 2
            sg = self.need(self.S((lambda e, sl=sl, sgsl=sgsl: e.activation(
                sb["SCRB"][:, ts_(sgsl, 512)], sb["SCRF"][:, ts_(sl, 512)], ACT.Silu,
                scale=sb["swcol"][:, 2:3])), waits=[d1, self.scrb_free.get(100 + sgsl)], label="gsilu"))
            self.bank_free["p0"] = d1
            d2 = self.need(self.V((lambda e, sl=sl: e.tensor_tensor(
                sb["SCRF"][:, ts_(sl, 512)], ps["p1"][:, :], sb["SCRF"][:, ts_(6, 512)], ALU.mult)),
                waits=[mm_j[1], mmx1], label="udeq"))
            self.bank_free["p1"] = d2
            guard = self.u_free[m] if m < 16 else self.z_free[m - 16]
            gu = self.need(self.V((lambda e, m=m, sl=sl, sgsl=sgsl: e.scalar_tensor_tensor(
                out=sb["A1" if m < 16 else "A2"][:, ts_(m if m < 16 else m - 16, 512)],
                in0=sb["SCRF"][:, ts_(sl, 512)], scalar=sb["swcol"][:, 5:6],
                in1=sb["SCRB"][:, ts_(sgsl, 512)], op0=ALU.mult, op1=ALU.mult)),
                waits=[d2, sg, guard], label="gu"))
            gutok[m] = gu
            self.scrt_free[m % 2] = gu
            self.scrb_free[100 + sgsl] = gu
        def gu_src(i):
            return sb["A1" if i < 16 else "A2"][:, ts_(i if i < 16 else i - 16, 512)]

        def xi2_dst(i):
            return sb["A3" if i < 16 else "A4"][:, ts_(i if i < 16 else i - 16, 512)]

        if self.cfg.debug_taps and c == 0:
            tg = self.prog.dma("sync", lambda e: e.dma_start(out=self.dram["tap_gu"], in_=sb["A1"][:, :]),
                               waits=gutok, label="tap", sem="dq_pro")
            self.prog.need(tg)
        xi2_guards = [self.brep_free[i] if i < 16 else self.crep_free[i - 16] for i in range(32)]
        self.scrf67_free = gutok[-1]
        xit2, mmx2, mrc2 = act_quant(gu_src, 32, gutok, "p3", "p5", xi2_dst, xi2_guards, "aq2")
        if self.cfg.debug_taps and c == 0:
            tg2 = self.prog.dma("sync", lambda e: e.dma_start(out=self.dram["tap_xi2"], in_=sb["A3"][:, :]),
                                waits=xit2, label="tap", sem="dq_pro")
            self.prog.need(tg2)
        xrtok = [None] * 8
        for r in range(8):
            views = []
            for g in range(2):
                v, rd, rl = self.ring.load("w_down", g * 2048, 16, r, 1, first, quant=7,
                                           extra_waits=[self.tok_bq["w_down"]] if first else [])
                views.append((v, rd, rl))
            bank = "p%d" % (r % 2)
            pairs = []
            for kk in range(32):
                pairs.append(((lambda kk=kk, views=views: views[kk // 16][0](kk % 16, 0)),
                              (lambda kk=kk: xi2_dst(kk)),
                              [xit2[kk]] + (views[kk // 16][1] if kk % 16 == 0 else [])))
            mm = self.need(self.mm_group(ps[bank][:, :], pairs,
                                         waits=[self.bank_free.get(bank)], label="downMM"))
            for g in range(2):
                views[g][2](mm)
            sl = 2 + r % 2
            d1 = self.need(self.V((lambda e, sl=sl, b=bank: e.tensor_tensor(
                sb["SCRF"][:, ts_(sl, 512)], ps[b][:, :], sb["SCRF"][:, ts_(6, 512)], ALU.mult)),
                waits=[mm, mmx2, self.scrt_free[r % 2]], label="ddeq"))
            ev = self.need(self.V((lambda e, r=r, sl=sl: e.scalar_tensor_tensor(
                out=sb["xres"][:, ts_(r, 512)], in0=sb["SCRF"][:, ts_(sl, 512)],
                scalar=sb["swcol"][:, 8:9], in1=sb["xres"][:, ts_(r, 512)],
                op0=ALU.mult, op1=ALU.add)), waits=[d1, xr[r]], label="res_add4"))
            xrtok[r] = ev
            self.scrt_free[r % 2] = ev
            self.bank_free[bank] = ev
        for q in range(16):
            self.u_free[q] = xrtok[-1]
            self.z_free[q] = xrtok[-1]
        for n in range(16):
            self.brep_free[n] = xrtok[-1]
            self.crep_free[n] = xrtok[-1]
        for r in range(8):
            self.a5_free[r] = xrtok[-1]
        self.yg_free = xrtok[-1]
        self.scrf67_free = xrtok[-1]
        self.y_free = xrtok[-1]
        self.bank_free["p4"] = xrtok[-1]
        self.bank_free["p5"] = xrtok[-1]
        self.bank_free["p3"] = xrtok[-1]
        return xrtok


def build_nc(cfg):
    b = Builder(cfg)
    nc = b.build()
    return nc, b


def kernel_run(ins, cfg=None, cores=(0,), trace=False):
    from concourse.bass_utils import run_bass_kernel_spmd
    cfg = cfg or Cfg()
    nc, b = build_nc(cfg)
    in_maps = [pack_inputs(ins, cfg, core) for core in cores]
    res = run_bass_kernel_spmd(nc, in_maps, core_ids=list(range(len(cores))), trace=trace)
    outs = []
    for r in res.results:
        outs.append(np.ascontiguousarray(r["out_fm"].T))
    return outs, res



def kernel(**inputs):
    from concourse.bass_utils import run_bass_kernel_spmd
    cfg = Cfg(L=4096, Tc=512, blocks=(0, 1, 2, 3))
    nc, b = build_nc(cfg)
    in_maps = [pack_inputs(inputs, cfg, core) for core in range(8)]
    res = run_bass_kernel_spmd(nc, in_maps, core_ids=list(range(8)))
    out = np.stack([np.ascontiguousarray(r["out_fm"].T) for r in res.results], axis=0)
    return out.astype(np.float32, copy=False)
